# version 02
# baseline (speedup 1.0000x reference)
import numpy as np
import ml_dtypes
import concourse.bass as bass
import concourse.mybir as mybir
from concourse.bass_utils import run_bass_kernel_spmd

F32 = mybir.dt.float32
BF16 = mybir.dt.bfloat16
AF = mybir.ActivationFunctionType
ALU = mybir.AluOpType
AX = mybir.AxisListType


_QUEUES = {"qsync": "sync", "qscalar": "scalar", "qgpsimd": "gpsimd"}
_ENGINES = ("tensor", "vector", "scalar", "gpsimd", "sync")


class Prog:
    def __init__(self, nc):
        self.nc = nc
        self.sems = {}
        self.cnt = {}
        self.clock = {e: {} for e in _ENGINES}
        self.snap = {}
        self.last_w = {}
        self.readers = {}
        self.plan = {e: [] for e in _ENGINES}
        self.n_ops = 0
        self.n_waits = 0

    def _sem(self, name):
        if name not in self.sems:
            self.sems[name] = self.nc.alloc_semaphore(name="s_" + name)
            self.cnt[name] = 0
        return self.sems[name]

    def op(self, stream, fn, reads=(), writes=(), chan=None, n=1):
        is_dma = stream in _QUEUES
        if is_dma:
            issuer = _QUEUES[stream]
            assert chan is not None
            sem_name, delta = "c_" + chan, 16 * n
        else:
            issuer = stream
            sem_name, delta = stream, 1
        self._sem(sem_name)

        deps = {}

        def add_dep(d):
            if d and deps.get(d[0], 0) < d[1]:
                deps[d[0]] = d[1]

        for b in reads:
            add_dep(self.last_w.get(b))
        for b in writes:
            add_dep(self.last_w.get(b))
            for s, v in self.readers.get(b, {}).items():
                add_dep((s, v))

        clock = self.clock[issuer]
        waits = []
        for dep in sorted(deps.items()):
            if (not is_dma) and stream == "tensor" and dep[0] == "tensor":
                self._absorb(clock, dep)
                continue
            if clock.get(dep[0], 0) < dep[1]:
                waits.append(dep)
                self._absorb(clock, dep)

        new_cnt = self.cnt[sem_name] + delta
        self.cnt[sem_name] = new_cnt
        me = (sem_name, new_cnt)
        self.snap[me] = dict(clock)
        for b in writes:
            self.last_w[b] = me
            self.readers[b] = {}
        for b in reads:
            self.readers.setdefault(b, {})[sem_name] = new_cnt

        self.plan[issuer].append((waits, fn, (sem_name, 16 if is_dma else 1), is_dma))
        self.n_ops += 1
        self.n_waits += len(waits)
        return me

    def _absorb(self, clock, dep):
        sem, val = dep
        if clock.get(sem, 0) >= val:
            return
        clock[sem] = val
        snap = self.snap.get((sem, val))
        if snap:
            for s2, v2 in snap.items():
                if clock.get(s2, 0) < v2:
                    clock[s2] = v2

    def barrier_val(self, chan):
        name = "c_" + chan if ("c_" + chan) in self.cnt else chan
        return self.cnt.get(name, 0)

    def final_wait(self, on_engine, chan, val):
        name = "c_" + chan if ("c_" + chan) in self.cnt else chan
        self.plan[on_engine].append(([(name, val)], None, None, False))

    def render(self):
        nc = self.nc
        with nc.Block() as block:
            for eng_name in _ENGINES:
                ops = self.plan[eng_name]
                if not ops:
                    continue

                def make(ops=ops):
                    def body(eng):
                        for waits, fn, inc, is_dma in ops:
                            for (s, v) in waits:
                                eng.wait_ge(self.sems[s], v)
                            if fn is None:
                                continue
                            res = fn(eng)
                            if inc is None:
                                continue
                            if is_dma:
                                insts = res if isinstance(res, (list, tuple)) else [res]
                                for inst in insts:
                                    inst.then_inc(self.sems[inc[0]], 16)
                            else:
                                res.then_inc(self.sems[inc[0]], 1)
                    return body

                getattr(block, eng_name)(make())


D = 1024
DI = 2048
DS = 16
DTR = 64
DC = 4
E = 8
FF = 3584
KD = D // 128
KI = DI // 128
KF = FF // 128
C = 512
EPS = 1e-5


class MK:

    def __init__(self, nc, T):
        self.nc = nc
        self.T = T
        self.NCH = T // C
        self.p = Prog(nc)
        self.S = {}
        self.P = []
        self._declare_dram()
        self._alloc_sbuf()

    def _declare_dram(self):
        nc = self.nc
        dt = nc.dram_tensor
        self.w_inT = dt("w_inT", [D, 2 * DI], F32, kind="ExternalInput").ap()
        self.w_outT = dt("w_outT", [DI, D], F32, kind="ExternalInput").ap()
        self.w_xT_l = dt("w_xT_l", [128, KI * 96], F32, kind="ExternalInput").ap()
        self.w_dtT = dt("w_dtT", [DTR, DI], F32, kind="ExternalInput").ap()
        self.rw_l = dt("rw_l", [128, KD * E], F32, kind="ExternalInput").ap()
        self.conv_w_l = dt("conv_w_l", [128, KI * DC], F32, kind="ExternalInput").ap()
        self.conv_b_l = dt("conv_b_l", [128, KI], F32, kind="ExternalInput").ap()
        self.Dvec_l = dt("Dvec_l", [128, KI], F32, kind="ExternalInput").ap()
        self.bdt_l = dt("bdt_l", [128, KI], F32, kind="ExternalInput").ap()
        self.alog = dt("alog", [1, DS], F32, kind="ExternalInput").ap()
        self.ident_in = dt("ident_in", [128, 128], F32, kind="ExternalInput").ap()
        self.sel_in = dt("sel_in", [DS, DS * 128], BF16, kind="ExternalInput").ap()
        self.wg_in = [dt(f"wg{e}", [D, FF], F32, kind="ExternalInput").ap() for e in range(E)]
        self.wu_in = [dt(f"wu{e}", [D, FF], F32, kind="ExternalInput").ap() for e in range(E)]
        self.wd_in = [dt(f"wd{e}", [FF, D], F32, kind="ExternalInput").ap() for e in range(E)]
        dti = lambda n, s: dt(n, s, BF16, kind="Internal").ap()
        self.wi_bf = dti("wi_bf", [16 * 128, KD * 256])
        self.wo_bf = dti("wo_bf", [KD * 128, KI * 128])
        self.wg_bf = [dti(f"wg_bf{e}", [14 * 128, KD * 256]) for e in range(E)]
        self.wu_bf = [dti(f"wu_bf{e}", [14 * 128, KD * 256]) for e in range(E)]
        self.wd_bf = [dti(f"wd_bf{e}", [FF, D]) for e in range(E)]
        self.out_til = dt("out_til", [self.NCH * 128, KD * C], F32, kind="ExternalOutput").ap()
        self.x_til = dt("x_til", [self.NCH * 128, KD * C], F32, kind="ExternalInput").ap()

    def _alloc_sbuf(self):
        nc = self.nc
        S = self.S

        def sb(name, shape, dtype):
            S[name] = nc.alloc_sbuf_tensor(name, shape, dtype)

        sb("ones_row_f", [1, 128], F32)
        sb("ones_col_bf", [128, 1], BF16)
        sb("ident", [128, 128], F32)
        sb("sel", [DS, DS * 128], BF16)
        sb("conv_w", [128, KI * DC], F32)
        sb("conv_b", [128, KI], F32)
        sb("Dvec", [128, KI], F32)
        sb("bdt", [128, KI], F32)
        sb("alog_sb", [1, DS], F32)
        sb("a_bc", [128, DS], F32)
        sb("cconst", [128, 3], F32)
        sb("mpad1", [128, 8], F32)
        sb("mpad2", [128, 8], F32)
        sb("w_x_sb", [128, KI * 96], BF16)
        sb("w_dt_sb", [DTR, DI], BF16)
        sb("rw_sb", [128, KD * E], BF16)
        for par in range(2):
            sb(f"xb{par}", [128, C], BF16)
        for par in range(2):
            sb(f"xnb{par}", [128, KD * C], BF16)
        sb("xnb_u", [128, KD * C], BF16)
        sb("m_row", [1, C], F32)
        sb("var_row", [1, C], F32)
        sb("rs_row", [1, C], F32)
        sb("mrs_row", [1, C], F32)
        sb("lntmp", [128, C], F32)
        sb("lsb", [E, C], F32)
        sb("ltok", [128, 4 * E], F32)
        sb("l2t", [128, 4 * E], F32)
        sb("l3t", [128, 4 * E], F32)
        sb("eq1", [128, 4 * E], F32)
        sb("eq2", [128, 4 * E], F32)
        sb("cmb_tok", [128, 4 * E], F32)
        sb("rm1", [128, 4], F32)
        sb("rm2", [128, 4], F32)
        sb("rd_", [128, 4], F32)
        sb("re_", [128, 4], F32)
        sb("rw1", [128, 4], F32)
        sb("rw2", [128, 4], F32)
        sb("cmb_rows_bf", [E, C], BF16)
        for par in range(2):
            sb(f"cmb_sb{par}", [128, C], BF16)
        for par in range(2):
            sb(f"win_slot{par}", [128, KD * 256], BF16)
        sb("xr", [128, KI * (C + 3)], BF16)
        sb("zs", [128, KI * C], BF16)
        sb("xs", [128, KI * C], BF16)
        sb("convacc", [128, C], F32)
        sb("dtr_b", [DTR, C], BF16)
        sb("brow_bf", [DS, C], BF16)
        sb("crow_bf", [DS, C], BF16)
        for par in range(2):
            sb(f"dt_f{par}", [128, C], F32)
            sb(f"dtx{par}", [128, C], BF16)
            sb(f"y_sb{par}", [128, C], F32)
        sb("u_sb", [128, C], F32)
        sb("hs_sb", [128, C], F32)
        sb("ys", [128, KI * C], BF16)
        sb("h_all", [128, KI * DS], F32)
        for par in range(2):
            sb(f"wout_slot{par}", [128, KI * 128], BF16)
        sb("out_acc", [128, KD * C], F32)
        for par in range(2):
            sb(f"wg_slot{par}", [128, KD * 256], BF16)
            sb(f"wu_slot{par}", [128, KD * 256], BF16)
            sb(f"wd_slot{par}", [128, D], BF16)
            sb(f"gs_sb{par}", [128, C], BF16)
        sb("h_half", [128, 14 * C], BF16)
        for j in range(8):
            self.P.append(nc.alloc_psum_tensor(f"ps{j}", [128, C], F32))

    def mm_group(self, mms, reads, writes):
        def fn(e):
            last = None
            for out, lhsT, rhs, st, sp in mms:
                last = e.matmul(out, lhsT, rhs, start=st, stop=sp)
            return last
        self.p.op("tensor", fn, reads=reads, writes=writes)

    def xcol(self, name, i, w=C):
        return self.S[name][:, i * w:(i + 1) * w]

    def prologue(self):
        p, S = self.p, self.S

        wi4_dst = self.wi_bf.rearrange("(g p) (k m) -> g p k m", p=128, m=256)
        wi4_src = self.w_inT.rearrange("(k p) (g m) -> g p k m", p=128, m=256)
        for g in range(16):
            p.op("qgpsimd", lambda e, g=g: [e.dma_start(out=wi4_dst[g], in_=wi4_src[g])],
                 writes=("hbm_wi",), chan="wi")
        wo4_dst = self.wo_bf.rearrange("(m p) (k c) -> m p k c", p=128, c=128)
        wo4_src = self.w_outT.rearrange("(k p) (m c) -> m p k c", p=128, c=128)
        for mm_ in range(KD):
            p.op("qgpsimd", lambda e, mm_=mm_: [e.dma_start(out=wo4_dst[mm_], in_=wo4_src[mm_])],
                 writes=("hbm_wo",), chan="wo")
        for e_ in range(E):
            g_dst = self.wg_bf[e_].rearrange("(s p) (k m) -> s p k m", p=128, m=256)
            g_src = self.wg_in[e_].rearrange("(k p) (s m) -> s p k m", p=128, m=256)
            u_dst = self.wu_bf[e_].rearrange("(s p) (k m) -> s p k m", p=128, m=256)
            u_src = self.wu_in[e_].rearrange("(k p) (s m) -> s p k m", p=128, m=256)
            for s in range(14):
                p.op("qgpsimd", lambda e, s=s, g_dst=g_dst, g_src=g_src: [
                    e.dma_start(out=g_dst[s], in_=g_src[s])],
                    writes=(f"hbm_cg{e_}",), chan=f"cg{e_}")
                p.op("qgpsimd", lambda e, s=s, u_dst=u_dst, u_src=u_src: [
                    e.dma_start(out=u_dst[s], in_=u_src[s])],
                    writes=(f"hbm_cu{e_}",), chan=f"cu{e_}")
            p.op("qgpsimd", lambda e, e_=e_: [e.dma_start(out=self.wd_bf[e_][:], in_=self.wd_in[e_][:])],
                 writes=(f"hbm_cd{e_}",), chan=f"cd{e_}")

        loads = [("ident", self.ident_in), ("sel", self.sel_in), ("conv_w", self.conv_w_l),
                 ("conv_b", self.conv_b_l), ("Dvec", self.Dvec_l), ("bdt", self.bdt_l),
                 ("alog_sb", self.alog)]
        for nm, src in loads:
            p.op("qsync", lambda e, nm=nm, src=src: [e.dma_start(out=S[nm][:], in_=src[:])],
                 writes=(nm,), chan="cst_" + nm)
        p.op("qsync", lambda e: [e.dma_start(out=S["out_acc"][:, 0:KI * 96], in_=self.w_xT_l[:])],
             writes=(("oa", 0), ("oa", 1), ("oa", 2)), chan="cst_wx")
        p.op("qsync", lambda e: [e.dma_start(out=S["out_acc"][0:DTR, KI * 96:KI * 96 + DI], in_=self.w_dtT[:])],
             writes=(("oa", 3), ("oa", 4), ("oa", 5), ("oa", 6)), chan="cst_wdt")
        p.op("qsync", lambda e: [e.dma_start(out=S["out_acc"][:, 3648:3648 + KD * E], in_=self.rw_l[:])],
             writes=(("oa", 7),), chan="cst_rw")
        p.op("vector", lambda e: e.tensor_copy(S["w_x_sb"][:], S["out_acc"][:, 0:KI * 96]),
             reads=(("oa", 0), ("oa", 1), ("oa", 2)), writes=("w_x_sb",))
        p.op("vector", lambda e: e.tensor_copy(S["w_dt_sb"][:], S["out_acc"][0:DTR, KI * 96:KI * 96 + DI]),
             reads=(("oa", 3), ("oa", 4), ("oa", 5), ("oa", 6)), writes=("w_dt_sb",))
        p.op("vector", lambda e: e.tensor_copy(S["rw_sb"][:], S["out_acc"][:, 3648:3648 + KD * E]),
             reads=(("oa", 7),), writes=("rw_sb",))

        p.op("vector", lambda e: e.memset(S["ones_row_f"][:], 1.0), writes=("ones_row_f",))
        p.op("vector", lambda e: e.memset(S["ones_col_bf"][:], 1.0), writes=("ones_col_bf",))
        p.op("vector", lambda e: e.memset(S["cconst"][:, 0:1], EPS), writes=("cconst",))
        p.op("vector", lambda e: e.memset(S["cconst"][:, 1:2], 1.0), writes=("cconst",))
        p.op("vector", lambda e: e.memset(S["cconst"][:, 2:3], -0.5), writes=("cconst",))
        p.op("vector", lambda e: e.memset(S["mpad1"][:], 1e30), writes=("mpad1",))
        p.op("vector", lambda e: e.memset(S["mpad2"][:], 1e30), writes=("mpad2",))
        p.op("gpsimd", lambda e: e.memset(S["h_all"][:], 0.0),
             writes=tuple(("h", i, s) for i in range(KI) for s in range(DS)))
        p.op("gpsimd", lambda e: e.memset(S["xr"][:], 0.0),
             writes=tuple(("xr", i) for i in range(KI)))

        p.op("scalar", lambda e: e.activation(S["alog_sb"][:], S["alog_sb"][:], AF.Exp),
             reads=("alog_sb",), writes=("alog_sb",))
        p.op("tensor", lambda e: e.matmul(self.P[0][:, 0:DS], S["ones_row_f"][:], S["alog_sb"][:],
                                          start=True, stop=True),
             reads=("ones_row_f", "alog_sb"), writes=("ps0",))
        p.op("vector", lambda e: e.tensor_scalar(S["a_bc"][:], self.P[0][:, 0:DS], -1.0, None, ALU.mult),
             reads=("ps0",), writes=("a_bc",))

    def ln_phase(self, n):
        p, S, P = self.p, self.S, self.P
        t0 = n * C
        xnb = f"xnb{n % 2}"

        p.op("qsync", lambda e: [e.dma_start(
            out=S["out_acc"][:],
            in_=self.x_til[n * 128:(n + 1) * 128, :])],
            writes=tuple(("oa", k) for k in range(KD)), chan="x")

        for k in range(KD):
            xbp = f"xb{k % 2}"
            p.op("vector", lambda e, k=k, xbp=xbp: e.tensor_copy(S[xbp][:], self.xcol("out_acc", k)),
                 reads=(("oa", k),), writes=(xbp,))
            self.mm_group([(P[0][0:1, :], S["ones_col_bf"][:], S[xbp][:], k == 0, k == KD - 1)],
                          reads=(xbp, "ones_col_bf"), writes=("ps0",))
            p.op("scalar", lambda e, xbp=xbp: e.activation(S[xbp][:], S[xbp][:], AF.Square),
                 reads=(xbp,), writes=(xbp,))
            self.mm_group([(P[1][0:1, :], S["ones_col_bf"][:], S[xbp][:], k == 0, k == KD - 1)],
                          reads=(xbp, "ones_col_bf"), writes=("ps1",))
        p.op("vector", lambda e: e.tensor_scalar(S["m_row"][:], P[0][0:1, :], 1.0 / D, None, ALU.mult),
             reads=("ps0",), writes=("m_row",))
        p.op("vector", lambda e: e.tensor_mul(S["var_row"][:], S["m_row"][:], S["m_row"][:]),
             reads=("m_row",), writes=("var_row",))
        p.op("vector", lambda e: e.scalar_tensor_tensor(S["var_row"][:], P[1][0:1, :], 1.0 / D,
                                                        S["var_row"][:], ALU.mult, ALU.subtract),
             reads=("ps1", "var_row"), writes=("var_row",))
        p.op("scalar", lambda e: e.activation(S["rs_row"][:], S["var_row"][:], AF.Ln,
                                              bias=S["cconst"][0:1, 0:1]),
             reads=("var_row", "cconst"), writes=("rs_row",))
        p.op("scalar", lambda e: e.activation(S["rs_row"][:], S["rs_row"][:], AF.Exp,
                                              scale=S["cconst"][0:1, 2:3]),
             reads=("rs_row", "cconst"), writes=("rs_row",))
        p.op("vector", lambda e: e.scalar_tensor_tensor(S["mrs_row"][:], S["m_row"][:], -1.0,
                                                        S["rs_row"][:], ALU.mult, ALU.mult),
             reads=("m_row", "rs_row"), writes=("mrs_row",))
        p.op("tensor", lambda e: e.matmul(P[2][:], S["ones_row_f"][:], S["rs_row"][:], start=True, stop=True),
             reads=("ones_row_f", "rs_row"), writes=("ps2",))
        p.op("tensor", lambda e: e.matmul(P[3][:], S["ones_row_f"][:], S["mrs_row"][:], start=True, stop=True),
             reads=("ones_row_f", "mrs_row"), writes=("ps3",))
        for k in range(KD):
            p.op("vector", lambda e, k=k: e.tensor_tensor(S["lntmp"][:], self.xcol("out_acc", k),
                                                          P[2][:], ALU.mult),
                 reads=(("oa", k), "ps2"), writes=("lntmp",))
            p.op("vector", lambda e, k=k: e.tensor_tensor(self.xcol(xnb, k), S["lntmp"][:],
                                                          P[3][:], ALU.add),
                 reads=("lntmp", "ps3"), writes=((xnb, k),))

    def router_phase(self, n):
        p, S, P = self.p, self.S, self.P
        xnb = f"xnb{n % 2}"
        self.mm_group([(P[4][0:E, :], S["rw_sb"][:, k * E:(k + 1) * E], self.xcol(xnb, k),
                        k == 0, k == KD - 1) for k in range(KD)],
                      reads=tuple((xnb, k) for k in range(KD)) + ("rw_sb",),
                      writes=("ps4",))
        p.op("scalar", lambda e: e.copy(S["lsb"][:], P[4][0:E, :]), reads=("ps4",), writes=("lsb",))
        for b in range(4):
            p.op("tensor", lambda e, b=b: e.transpose(P[5][:, 8 * b:8 * b + 8],
                                                      S["lsb"][:, 128 * b:128 * (b + 1)],
                                                      S["ident"][0:E, 0:E]),
                 reads=("lsb", "ident"), writes=("ps5",))
        p.op("vector", lambda e: e.tensor_copy(S["ltok"][:], P[5][:, 0:4 * E]),
             reads=("ps5",), writes=("ltok",))
        for b in range(4):
            sl = slice(8 * b, 8 * b + 8)
            cb = slice(b, b + 1)
            p.op("vector", lambda e, sl=sl, cb=cb: e.reduce_max(S["rm1"][:, cb], S["ltok"][:, sl], axis=AX.X),
                 reads=("ltok",), writes=("rm1",))
            p.op("vector", lambda e, cb=cb: e.tensor_copy(S["mpad1"][:, 0:1], S["rm1"][:, cb]),
                 reads=("rm1",), writes=("mpad1",))
            p.op("vector", lambda e, sl=sl: e.match_replace(S["l2t"][:, sl], S["mpad1"][:], S["ltok"][:, sl], -1e30),
                 reads=("mpad1", "ltok"), writes=("l2t",))
            p.op("vector", lambda e, sl=sl, cb=cb: e.reduce_max(S["rm2"][:, cb], S["l2t"][:, sl], axis=AX.X),
                 reads=("l2t",), writes=("rm2",))
            p.op("vector", lambda e, cb=cb: e.tensor_copy(S["mpad2"][:, 0:1], S["rm2"][:, cb]),
                 reads=("rm2",), writes=("mpad2",))
            p.op("vector", lambda e, sl=sl: e.match_replace(S["l3t"][:, sl], S["mpad2"][:], S["l2t"][:, sl], -1e30),
                 reads=("mpad2", "l2t"), writes=("l3t",))
        p.op("vector", lambda e: e.tensor_sub(S["rd_"][:], S["rm2"][:], S["rm1"][:]),
             reads=("rm2", "rm1"), writes=("rd_",))
        p.op("scalar", lambda e: e.activation(S["re_"][:], S["rd_"][:], AF.Exp),
             reads=("rd_",), writes=("re_",))
        p.op("vector", lambda e: e.tensor_scalar(S["rw1"][:], S["re_"][:], 1.0, None, ALU.add),
             reads=("re_",), writes=("rw1",))
        p.op("vector", lambda e: e.reciprocal(S["rw1"][:], S["rw1"][:]),
             reads=("rw1",), writes=("rw1",))
        p.op("vector", lambda e: e.tensor_scalar(S["rw2"][:], S["rw1"][:], -1.0, 1.0, ALU.mult, ALU.add),
             reads=("rw1",), writes=("rw2",))
        p.op("vector", lambda e: e.tensor_tensor(S["eq1"][:], S["ltok"][:], S["l2t"][:], ALU.not_equal),
             reads=("ltok", "l2t"), writes=("eq1",))
        p.op("vector", lambda e: e.tensor_tensor(S["eq2"][:], S["l2t"][:], S["l3t"][:], ALU.not_equal),
             reads=("l2t", "l3t"), writes=("eq2",))
        for b in range(4):
            sl = slice(8 * b, 8 * b + 8)
            cb = slice(b, b + 1)
            p.op("vector", lambda e, sl=sl, cb=cb: e.tensor_scalar(S["eq1"][:, sl], S["eq1"][:, sl],
                                                                   S["rw1"][:, cb], None, ALU.mult),
                 reads=("eq1", "rw1"), writes=("eq1",))
            p.op("vector", lambda e, sl=sl, cb=cb: e.tensor_scalar(S["eq2"][:, sl], S["eq2"][:, sl],
                                                                   S["rw2"][:, cb], None, ALU.mult),
                 reads=("eq2", "rw2"), writes=("eq2",))
        p.op("vector", lambda e: e.tensor_add(S["cmb_tok"][:], S["eq1"][:], S["eq2"][:]),
             reads=("eq1", "eq2"), writes=("cmb_tok",))
        for b in range(4):
            p.op("tensor", lambda e, b=b: e.transpose(P[6][0:E, 128 * b:128 * (b + 1)],
                                                      S["cmb_tok"][:, 8 * b:8 * b + 8],
                                                      S["ident"][:]),
                 reads=("cmb_tok", "ident"), writes=("ps6",))
        p.op("scalar", lambda e: e.copy(S["cmb_rows_bf"][:], P[6][0:E, :]),
             reads=("ps6",), writes=("cmb_rows_bf",))

    def front_phase(self, n):
        p, S, P = self.p, self.S, self.P
        xnb = f"xnb{n % 2}"
        if n > 0:
            for i in range(KI):
                o = i * (C + 3)
                p.op("gpsimd", lambda e, o=o: e.tensor_copy(S["xr"][:, o:o + 3], S["xr"][:, o + C:o + C + 3]),
                     reads=(("xr", i),), writes=(("xr", i),))
        for g in range(16):
            par = g % 2
            banks = (0, 1) if par == 0 else (2, 3)
            p.op("qsync", lambda e, g=g, par=par: [e.dma_start(
                out=S[f"win_slot{par}"][:],
                in_=self.wi_bf[g * 128:(g + 1) * 128, :])],
                reads=("hbm_wi",), writes=(("win", par),), chan=f"win{par}")
            for j in range(2):
                mt = 2 * g + j
                bank = banks[j]
                self.mm_group([(P[bank][:], S[f"win_slot{par}"][:, k * 256 + 128 * j:k * 256 + 128 * (j + 1)],
                                self.xcol(xnb, k), k == 0, k == KD - 1) for k in range(KD)],
                              reads=tuple((xnb, k) for k in range(KD)) + (("win", par),),
                              writes=(f"ps{bank}",))
                if mt < KI:
                    o = mt * (C + 3)
                    p.op("scalar", lambda e, mt=mt, bank=bank, o=o: e.copy(S["xr"][:, o + 3:o + 3 + C], P[bank][:]),
                         reads=(f"ps{bank}",), writes=(("xr", mt),))
                else:
                    zt = mt - KI
                    p.op("scalar", lambda e, zt=zt, bank=bank: e.activation(self.xcol("zs", zt), P[bank][:], AF.Silu),
                         reads=(f"ps{bank}",), writes=(("zs", zt),))
        for i in range(KI):
            o = i * (C + 3)
            ca = "convacc"
            p.op("vector", lambda e, o=o, i=i, ca=ca: e.tensor_scalar(
                S[ca][:], S["xr"][:, o:o + C], S["conv_w"][:, i * DC:i * DC + 1], None, ALU.mult),
                reads=(("xr", i), "conv_w"), writes=(ca,))
            for kk in range(1, DC):
                p.op("vector", lambda e, o=o, i=i, kk=kk, ca=ca: e.scalar_tensor_tensor(
                    S[ca][:], S["xr"][:, o + kk:o + kk + C], S["conv_w"][:, i * DC + kk:i * DC + kk + 1],
                    S[ca][:], ALU.mult, ALU.add),
                    reads=(("xr", i), "conv_w", ca), writes=(ca,))
            p.op("scalar", lambda e, i=i, ca=ca: e.activation(self.xcol("xs", i), S[ca][:], AF.Silu,
                                                              bias=S["conv_b"][:, i:i + 1]),
                 reads=(ca, "conv_b"), writes=(("xs", i),))
        xs_reads = tuple(("xs", i) for i in range(KI))
        self.mm_group([(P[4][0:DTR, :], S["w_x_sb"][:, k * 96:k * 96 + DTR], self.xcol("xs", k),
                        k == 0, k == KI - 1) for k in range(KI)],
                      reads=xs_reads + ("w_x_sb",), writes=("ps4",))
        self.mm_group([(P[5][0:DS, :], S["w_x_sb"][:, k * 96 + DTR:k * 96 + DTR + DS], self.xcol("xs", k),
                        k == 0, k == KI - 1) for k in range(KI)],
                      reads=xs_reads + ("w_x_sb",), writes=("ps5",))
        self.mm_group([(P[5][32:32 + DS, :], S["w_x_sb"][:, k * 96 + 80:k * 96 + 96], self.xcol("xs", k),
                        k == 0, k == KI - 1) for k in range(KI)],
                      reads=xs_reads + ("w_x_sb",), writes=("ps5",))
        p.op("scalar", lambda e: e.copy(S["dtr_b"][:], P[4][0:DTR, :]), reads=("ps4",), writes=("dtr_b",))
        p.op("scalar", lambda e: e.copy(S["brow_bf"][:], P[5][0:DS, :]), reads=("ps5",), writes=("brow_bf",))
        p.op("scalar", lambda e: e.copy(S["crow_bf"][:], P[5][32:32 + DS, :]), reads=("ps5",), writes=("crow_bf",))

    def scan_dt_unit(self, pr):
        p, S, P = self.p, self.S, self.P
        for i in (2 * pr, 2 * pr + 1):
            par = i % 2
            p.op("tensor", lambda e, i=i: e.matmul(P[7][:], S["w_dt_sb"][:, 128 * i:128 * (i + 1)],
                                                   S["dtr_b"][:], start=True, stop=True),
                 reads=("w_dt_sb", "dtr_b"), writes=("ps7",))
            p.op("scalar", lambda e, i=i, par=par: e.activation(S[f"dt_f{par}"][:], P[7][:], AF.Exp,
                                                                bias=S["bdt"][:, i:i + 1]),
                 reads=("ps7", "bdt"), writes=(f"dt_f{par}",))
            p.op("scalar", lambda e, par=par: e.activation(S[f"dt_f{par}"][:], S[f"dt_f{par}"][:], AF.Ln,
                                                           bias=S["cconst"][:, 1:2]),
                 reads=(f"dt_f{par}", "cconst"), writes=(f"dt_f{par}",))
            p.op("vector", lambda e, i=i, par=par: e.tensor_tensor(S[f"dtx{par}"][:], S[f"dt_f{par}"][:],
                                                                   self.xcol("xs", i), ALU.mult),
                 reads=(f"dt_f{par}", ("xs", i)), writes=(f"dtx{par}",))
            p.op("vector", lambda e, i=i, par=par: e.tensor_scalar(S[f"y_sb{par}"][:], self.xcol("xs", i),
                                                                   S["Dvec"][:, i:i + 1], None, ALU.mult),
                 reads=(("xs", i), "Dvec"), writes=(f"y_sb{par}",))

    def scan_s_unit(self, pr, s):
        p, S, P = self.p, self.S, self.P
        p.op("tensor", lambda e, s=s: e.matmul(P[5][:], S["sel"][:, 128 * s:128 * (s + 1)],
                                               S["brow_bf"][:], start=True, stop=True),
             reads=("sel", "brow_bf"), writes=("ps5",))
        p.op("tensor", lambda e, s=s: e.matmul(P[6][:], S["sel"][:, 128 * s:128 * (s + 1)],
                                               S["crow_bf"][:], start=True, stop=True),
             reads=("sel", "crow_bf"), writes=("ps6",))
        for i in (2 * pr, 2 * pr + 1):
            par = i % 2
            hkey = ("h", i, s)
            hcol = i * DS + s
            ab = P[4] if par == 0 else P[7]
            abk = "ps4" if par == 0 else "ps7"
            p.op("scalar", lambda e, par=par, s=s, ab=ab: e.activation(ab[:], S[f"dt_f{par}"][:], AF.Exp,
                                                                       scale=S["a_bc"][:, s:s + 1]),
                 reads=(f"dt_f{par}", "a_bc"), writes=(abk,))
            p.op("vector", lambda e, par=par: e.tensor_tensor(S["u_sb"][:], S[f"dtx{par}"][:], P[5][:], ALU.mult),
                 reads=(f"dtx{par}", "ps5"), writes=("u_sb",))
            p.op("vector", lambda e, par=par, hcol=hcol, ab=ab: e.tensor_tensor_scan(
                S["hs_sb"][:], ab[:], S["u_sb"][:], S["h_all"][:, hcol:hcol + 1],
                ALU.mult, ALU.add),
                reads=(abk, "u_sb", hkey), writes=("hs_sb",))
            p.op("gpsimd", lambda e, hcol=hcol: e.tensor_copy(S["h_all"][:, hcol:hcol + 1], S["hs_sb"][:, C - 1:C]),
                 reads=("hs_sb",), writes=(hkey,))
            p.op("vector", lambda e: e.tensor_tensor(S["u_sb"][:], S["hs_sb"][:], P[6][:], ALU.mult),
                 reads=("hs_sb", "ps6"), writes=("u_sb",))
            p.op("vector", lambda e, par=par: e.tensor_tensor(S[f"y_sb{par}"][:], S[f"y_sb{par}"][:],
                                                              S["u_sb"][:], ALU.add),
                 reads=(f"y_sb{par}", "u_sb"), writes=(f"y_sb{par}",))

    def scan_ys_unit(self, pr):
        p, S = self.p, self.S
        for i in (2 * pr, 2 * pr + 1):
            par = i % 2
            p.op("vector", lambda e, i=i, par=par: e.tensor_tensor(self.xcol("ys", i), S[f"y_sb{par}"][:],
                                                                   self.xcol("zs", i), ALU.mult),
                 reads=(f"y_sb{par}", ("zs", i)), writes=(("ys", i),))

    def moe_pre_unit(self, n, e_):
        p, S, P = self.p, self.S, self.P
        xnb = f"xnb{n % 2}"
        par = e_ % 2
        p.op("tensor", lambda e, e_=e_: e.matmul(P[4][:], S["sel"][0:E, 128 * e_:128 * (e_ + 1)],
                                                 S["cmb_rows_bf"][:], start=True, stop=True),
             reads=("sel", "cmb_rows_bf"), writes=("ps4",))
        p.op("scalar", lambda e, par=par: e.copy(S[f"cmb_sb{par}"][:], P[4][:]),
             reads=("ps4",), writes=(f"cmb_sb{par}",))
        for k in range(KD):
            p.op("vector", lambda e, k=k, par=par, xnb=xnb: e.tensor_tensor(
                self.xcol("xnb_u", k), self.xcol(xnb, k), S[f"cmb_sb{par}"][:], ALU.mult),
                reads=((xnb, k), f"cmb_sb{par}"), writes=(("xnu", k),))

    def moe_gu_unit(self, n, e_, h, st):
        p, S, P = self.p, self.S, self.P
        xnb = f"xnb{n % 2}"
        par = st % 2
        sidx = h * 7 + st
        p.op("qscalar", lambda e, sidx=sidx, par=par, e_=e_: [e.dma_start(
            out=S[f"wg_slot{par}"][:],
            in_=self.wg_bf[e_][sidx * 128:(sidx + 1) * 128, :])],
            reads=("hbm_cg%d" % e_,), writes=(("wgs", par),), chan=f"wg{par}")
        p.op("qscalar", lambda e, sidx=sidx, par=par, e_=e_: [e.dma_start(
            out=S[f"wu_slot{par}"][:],
            in_=self.wu_bf[e_][sidx * 128:(sidx + 1) * 128, :])],
            reads=("hbm_cu%d" % e_,), writes=(("wus", par),), chan=f"wu{par}")
        for j in range(2):
            ftl = st * 2 + j
            self.mm_group([(P[j][:], S[f"wg_slot{par}"][:, k * 256 + 128 * j:k * 256 + 128 * (j + 1)],
                            self.xcol(xnb, k), k == 0, k == KD - 1) for k in range(KD)],
                          reads=tuple((xnb, k) for k in range(KD)) + (("wgs", par),),
                          writes=(f"ps{j}",))
            self.mm_group([(P[2 + j][:], S[f"wu_slot{par}"][:, k * 256 + 128 * j:k * 256 + 128 * (j + 1)],
                            self.xcol("xnb_u", k), k == 0, k == KD - 1) for k in range(KD)],
                          reads=tuple(("xnu", k) for k in range(KD)) + (("wus", par),),
                          writes=(f"ps{2 + j}",))
            p.op("scalar", lambda e, j=j: e.activation(S[f"gs_sb{j}"][:], P[j][:], AF.Silu),
                 reads=(f"ps{j}",), writes=(f"gs_sb{j}",))
            p.op("vector", lambda e, j=j, ftl=ftl: e.tensor_tensor(self.xcol("h_half", ftl), S[f"gs_sb{j}"][:],
                                                                   P[2 + j][:], ALU.mult),
                 reads=(f"gs_sb{j}", f"ps{2 + j}"), writes=(("hh", ftl),))

    def moe_down_units(self, e_, h):
        p, S, P = self.p, self.S, self.P
        units = []
        for ks in range(14):
            def unit(ks=ks, e_=e_, h=h):
                par = ks % 2
                s0 = h * 14 + ks
                p.op("qsync", lambda e, s0=s0, par=par, e_=e_: [e.dma_start(
                    out=S[f"wd_slot{par}"][:],
                    in_=self.wd_bf[e_][128 * s0:128 * (s0 + 1), :])],
                    reads=("hbm_cd%d" % e_,), writes=(("wds", par),), chan=f"wd{par}")
                for grp in range(2):
                    mlist = range(4 * grp, 4 * grp + 4)
                    self.mm_group(
                        [(P[m][:], S[f"wd_slot{par}"][:, 128 * m:128 * (m + 1)],
                          self.xcol("h_half", ks), ks == 0, ks == 13)
                         for m in mlist],
                        reads=(("wds", par), ("hh", ks)),
                        writes=tuple(f"ps{m}" for m in mlist))
            units.append(unit)

        def evac(e_=e_, h=h):
            for m in range(8):
                p.op("vector", lambda e, m=m: e.tensor_tensor(self.xcol("out_acc", m), self.xcol("out_acc", m),
                                                              P[m][:], ALU.add),
                     reads=(("oa", m), f"ps{m}"), writes=(("oa", m),))
        units.append(evac)
        return units

    def wout_phase(self, n):
        p, S, P = self.p, self.S, self.P
        for m in range(KD):
            par = m % 2
            p.op("qsync", lambda e, m=m, par=par: [e.dma_start(
                out=S[f"wout_slot{par}"][:],
                in_=self.wo_bf[m * 128:(m + 1) * 128, :])],
                reads=("hbm_wo",), writes=(("wos", par),), chan=f"wout{par}")
            self.mm_group([(P[par][:], S[f"wout_slot{par}"][:, k * 128:(k + 1) * 128],
                            self.xcol("ys", k), k == 0, k == KI - 1) for k in range(KI)],
                          reads=tuple(("ys", k) for k in range(KI)) + (("wos", par),),
                          writes=(f"ps{par}",))
            p.op("vector", lambda e, m=m, par=par: e.tensor_tensor(self.xcol("out_acc", m),
                                                                   self.xcol("out_acc", m), P[par][:], ALU.add),
                 reads=(("oa", m), f"ps{par}"), writes=(("oa", m),))

    def store_phase(self, n):
        p, S = self.p, self.S
        p.op("qsync", lambda e: [e.dma_start(
            out=self.out_til[n * 128:(n + 1) * 128, :],
            in_=S["out_acc"][:])],
            reads=tuple(("oa", k) for k in range(KD)), writes=("stored",), chan="st")

    def chunk(self, n):
        self.ln_phase(n)
        self.router_phase(n)
        self.front_phase(n)
        for e_ in range(E):
            pr = e_
            self.moe_pre_unit(n, e_)
            self.scan_dt_unit(pr)
            for h in range(2):
                gus = [lambda st=st, h=h, e_=e_: self.moe_gu_unit(n, e_, h, st) for st in range(7)]
                ss = [lambda s=s: self.scan_s_unit(pr, s) for s in range(h * 8, h * 8 + 8)]
                ia, ib = 0, 0
                while ia < len(gus) or ib < len(ss):
                    if ia < len(gus):
                        gus[ia]()
                        ia += 1
                    if ib < len(ss):
                        ss[ib]()
                        ib += 1
                for u in self.moe_down_units(e_, h):
                    u()
            self.scan_ys_unit(pr)
        self.wout_phase(n)
        self.store_phase(n)

    def build(self):
        self.prologue()
        for n in range(self.NCH):
            self.chunk(n)
        self.p.final_wait("sync", "st", self.p.barrier_val("st"))
        self.p.render()



def _host_consts():
    ident = np.eye(128, dtype=np.float32)
    sel = np.zeros((DS, DS * 128), dtype=np.float32)
    for s in range(DS):
        sel[s, 128 * s:128 * (s + 1)] = 1.0
    sel = sel.astype(ml_dtypes.bfloat16)
    return ident, sel


def _host_prep(inputs, T):
    W_in = inputs["L0_attn_W_in"]
    conv_w = inputs["L0_attn_conv_w"]
    conv_b = inputs["L0_attn_conv_b"]
    A_log = inputs["L0_attn_A_log"]
    Dv = inputs["L0_attn_D"]
    W_x = inputs["L0_attn_W_x"]
    W_dt = inputs["L0_attn_W_dt"]
    b_dt = inputs["L0_attn_b_dt"]
    W_out = inputs["L0_attn_W_out"]
    router_w = inputs["L0_ffn_router_w"]
    wg = inputs["L0_ffn_w_gate"]
    wu = inputs["L0_ffn_w_up"]
    wd = inputs["L0_ffn_w_down"]
    ident, sel = _host_consts()

    f32 = np.float32
    shared = {
        "w_inT": np.ascontiguousarray(W_in.T, dtype=f32),
        "w_outT": np.ascontiguousarray(W_out.T, dtype=f32),
        "w_xT_l": np.ascontiguousarray(
            W_x.T.reshape(KI, 128, 96).transpose(1, 0, 2).reshape(128, KI * 96), dtype=f32),
        "w_dtT": np.ascontiguousarray(W_dt.T, dtype=f32),
        "rw_l": np.ascontiguousarray(
            router_w.T.reshape(KD, 128, E).transpose(1, 0, 2).reshape(128, KD * E), dtype=f32),
        "conv_w_l": np.ascontiguousarray(
            conv_w.reshape(KI, 128, DC).transpose(1, 0, 2).reshape(128, KI * DC), dtype=f32),
        "conv_b_l": np.ascontiguousarray(conv_b.reshape(KI, 128).T, dtype=f32),
        "Dvec_l": np.ascontiguousarray(Dv.reshape(KI, 128).T, dtype=f32),
        "bdt_l": np.ascontiguousarray(b_dt.reshape(KI, 128).T, dtype=f32),
        "alog": np.ascontiguousarray(A_log.reshape(1, DS), dtype=f32),
        "ident_in": ident,
        "sel_in": sel,
    }
    for e in range(E):
        shared[f"wg{e}"] = np.ascontiguousarray(wg[e], dtype=f32)
        shared[f"wu{e}"] = np.ascontiguousarray(wu[e], dtype=f32)
        shared[f"wd{e}"] = np.ascontiguousarray(wd[e], dtype=f32)
    return shared


def kernel(**inputs):
    x = np.asarray(inputs["L0_attn_x"])
    B, T, _ = x.shape
    assert B == 8 and T % C == 0

    nc = bass.Bass("TRN2", target_bir_lowering=False)
    mk = MK(nc, T)
    mk.build()

    shared = _host_prep(inputs, T)
    nch = T // C
    in_maps = []
    for b in range(B):
        m = dict(shared)
        m["x_til"] = _tile_x(x[b], nch)
        in_maps.append(m)

    res = run_bass_kernel_spmd(nc, in_maps, core_ids=list(range(B)))
    out = np.stack([_untile_out(np.asarray(res.results[b]["out_til"]), nch, T)
                    for b in range(B)], axis=0)
    return np.ascontiguousarray(out, dtype=np.float32)


def _tile_x(xb, nch):
    xT = xb.T
    return np.ascontiguousarray(
        xT.reshape(KD, 128, nch, C).transpose(2, 1, 0, 3).reshape(nch * 128, KD * C))


def _untile_out(ot, nch, T):
    outT = ot.reshape(nch, 128, KD, C).transpose(2, 1, 0, 3).reshape(D, T)
    return outT.T


# version 03
# speedup vs baseline: 1.1325x; 1.1325x over previous
import numpy as np
import ml_dtypes
import concourse.bass as bass
import concourse.mybir as mybir
from concourse.bass_utils import run_bass_kernel_spmd

F32 = mybir.dt.float32
BF16 = mybir.dt.bfloat16
AF = mybir.ActivationFunctionType
ALU = mybir.AluOpType
AX = mybir.AxisListType


_QUEUES = {"qsync": "sync", "qscalar": "scalar", "qgpsimd": "gpsimd"}
_ENGINES = ("tensor", "vector", "scalar", "gpsimd", "sync")


class Prog:
    def __init__(self, nc):
        self.nc = nc
        self.sems = {}
        self.cnt = {}
        self.clock = {e: {} for e in _ENGINES}
        self.snap = {}
        self.last_w = {}
        self.readers = {}
        self.plan = {e: [] for e in _ENGINES}
        self.n_ops = 0
        self.n_waits = 0

    def _sem(self, name):
        if name not in self.sems:
            self.sems[name] = self.nc.alloc_semaphore(name="s_" + name)
            self.cnt[name] = 0
        return self.sems[name]

    def op(self, stream, fn, reads=(), writes=(), chan=None, n=1):
        is_dma = stream in _QUEUES
        if is_dma:
            issuer = _QUEUES[stream]
            assert chan is not None
            sem_name, delta = "c_" + chan, 16 * n
        else:
            issuer = stream
            sem_name, delta = stream, 1
        self._sem(sem_name)

        deps = {}

        def add_dep(d):
            if d and deps.get(d[0], 0) < d[1]:
                deps[d[0]] = d[1]

        for b in reads:
            add_dep(self.last_w.get(b))
        for b in writes:
            add_dep(self.last_w.get(b))
            for s, v in self.readers.get(b, {}).items():
                add_dep((s, v))

        clock = self.clock[issuer]
        waits = []
        for dep in sorted(deps.items()):
            if (not is_dma) and stream == "tensor" and dep[0] == "tensor":
                self._absorb(clock, dep)
                continue
            if clock.get(dep[0], 0) < dep[1]:
                waits.append(dep)
                self._absorb(clock, dep)

        new_cnt = self.cnt[sem_name] + delta
        self.cnt[sem_name] = new_cnt
        me = (sem_name, new_cnt)
        self.snap[me] = dict(clock)
        for b in writes:
            self.last_w[b] = me
            self.readers[b] = {}
        for b in reads:
            self.readers.setdefault(b, {})[sem_name] = new_cnt

        self.plan[issuer].append((waits, fn, (sem_name, 16 if is_dma else 1), is_dma))
        self.n_ops += 1
        self.n_waits += len(waits)
        return me

    def _absorb(self, clock, dep):
        sem, val = dep
        if clock.get(sem, 0) >= val:
            return
        clock[sem] = val
        snap = self.snap.get((sem, val))
        if snap:
            for s2, v2 in snap.items():
                if clock.get(s2, 0) < v2:
                    clock[s2] = v2

    def barrier_val(self, chan):
        name = "c_" + chan if ("c_" + chan) in self.cnt else chan
        return self.cnt.get(name, 0)

    def final_wait(self, on_engine, chan, val):
        name = "c_" + chan if ("c_" + chan) in self.cnt else chan
        self.plan[on_engine].append(([(name, val)], None, None, False))

    def render(self):
        nc = self.nc
        with nc.Block() as block:
            for eng_name in _ENGINES:
                ops = self.plan[eng_name]
                if not ops:
                    continue

                def make(ops=ops):
                    def body(eng):
                        for waits, fn, inc, is_dma in ops:
                            for (s, v) in waits:
                                eng.wait_ge(self.sems[s], v)
                            if fn is None:
                                continue
                            res = fn(eng)
                            if inc is None:
                                continue
                            if is_dma:
                                insts = res if isinstance(res, (list, tuple)) else [res]
                                for inst in insts:
                                    inst.then_inc(self.sems[inc[0]], 16)
                            else:
                                res.then_inc(self.sems[inc[0]], 1)
                    return body

                getattr(block, eng_name)(make())


D = 1024
DI = 2048
DS = 16
DTR = 64
DC = 4
E = 8
FF = 3584
KD = D // 128
KI = DI // 128
KF = FF // 128
C = 512
EPS = 1e-5


class MK:

    def __init__(self, nc, T):
        self.nc = nc
        self.T = T
        self.NCH = T // C
        self.p = Prog(nc)
        self.S = {}
        self.P = []
        self._declare_dram()
        self._alloc_sbuf()

    def _declare_dram(self):
        nc = self.nc
        dt = nc.dram_tensor
        self.w_inT = dt("w_inT", [D, 2 * DI], F32, kind="ExternalInput").ap()
        self.w_outT = dt("w_outT", [DI, D], F32, kind="ExternalInput").ap()
        self.w_xT_l = dt("w_xT_l", [128, KI * 96], F32, kind="ExternalInput").ap()
        self.w_dtT = dt("w_dtT", [DTR, DI], F32, kind="ExternalInput").ap()
        self.rw_l = dt("rw_l", [128, KD * E], F32, kind="ExternalInput").ap()
        self.conv_w_l = dt("conv_w_l", [128, KI * DC], F32, kind="ExternalInput").ap()
        self.conv_b_l = dt("conv_b_l", [128, KI], F32, kind="ExternalInput").ap()
        self.Dvec_l = dt("Dvec_l", [128, KI], F32, kind="ExternalInput").ap()
        self.bdt_l = dt("bdt_l", [128, KI], F32, kind="ExternalInput").ap()
        self.alog = dt("alog", [1, DS], F32, kind="ExternalInput").ap()
        self.ident_in = dt("ident_in", [128, 128], F32, kind="ExternalInput").ap()
        self.sel_in = dt("sel_in", [DS, DS * 128], BF16, kind="ExternalInput").ap()
        self.wg_in = [dt(f"wg{e}", [D, FF], F32, kind="ExternalInput").ap() for e in range(E)]
        self.wu_in = [dt(f"wu{e}", [D, FF], F32, kind="ExternalInput").ap() for e in range(E)]
        self.wd_in = [dt(f"wd{e}", [FF, D], F32, kind="ExternalInput").ap() for e in range(E)]
        dti = lambda n, s: dt(n, s, BF16, kind="Internal").ap()
        self.wi_bf = dti("wi_bf", [32 * 128, KD * 128])
        self.wo_bf = dti("wo_bf", [KD * 128, KI * 128])
        self.wg_bf = [dti(f"wg_bf{e}", [14 * 128, KD * 256]) for e in range(E)]
        self.wu_bf = [dti(f"wu_bf{e}", [14 * 128, KD * 256]) for e in range(E)]
        self.wd_bf = [dti(f"wd_bf{e}", [FF, D]) for e in range(E)]
        self.out_til = dt("out_til", [self.NCH * 128, KD * C], F32, kind="ExternalOutput").ap()
        self.x_til = dt("x_til", [self.NCH * 128, KD * C], F32, kind="ExternalInput").ap()

    def _alloc_sbuf(self):
        nc = self.nc
        S = self.S

        def sb(name, shape, dtype):
            S[name] = nc.alloc_sbuf_tensor(name, shape, dtype)

        sb("ones_row_f", [1, 128], F32)
        sb("ones_col_bf", [128, 1], BF16)
        sb("ident", [128, 128], F32)
        sb("sel", [DS, DS * 128], BF16)
        sb("conv_w", [128, KI * DC], F32)
        sb("conv_b", [128, KI], F32)
        sb("Dvec", [128, KI], F32)
        sb("bdt", [128, KI], F32)
        sb("alog_sb", [1, DS], F32)
        sb("a_bc", [128, DS], F32)
        sb("cconst", [128, 3], F32)
        sb("mpad1", [128, 8], F32)
        sb("mpad2", [128, 8], F32)
        sb("w_x_sb", [128, KI * 96], BF16)
        sb("w_dt_sb", [DTR, DI], BF16)
        sb("rw_sb", [128, KD * E], BF16)
        sb("xb0", [128, C], BF16)
        for par in range(2):
            sb(f"xnb{par}", [128, KD * C], BF16)
        sb("xnb_u", [128, KD * C], BF16)
        sb("m_row", [1, C], F32)
        sb("var_row", [1, C], F32)
        sb("rs_row", [1, C], F32)
        sb("mrs_row", [1, C], F32)
        sb("lntmp", [128, C], F32)
        sb("ltok", [128, 4 * E], F32)
        sb("l2t", [128, 4 * E], F32)
        sb("l3t", [128, 4 * E], F32)
        sb("eq1", [128, 4 * E], F32)
        sb("eq2", [128, 4 * E], F32)
        sb("cmb_tok", [128, 4 * E], F32)
        sb("rm1", [128, 4], F32)
        sb("rm2", [128, 4], F32)
        sb("rd_", [128, 4], F32)
        sb("re_", [128, 4], F32)
        sb("rw1", [128, 4], F32)
        sb("rw2", [128, 4], F32)
        sb("cmb_rows_bf", [E, C], BF16)
        for par in range(2):
            sb(f"cmb_sb{par}", [128, C], BF16)
        for par in range(2):
            sb(f"win_slot{par}", [128, KD * 128], BF16)
        sb("xr", [128, KI * (C + 3)], BF16)
        sb("zs", [128, KI * C], BF16)
        sb("xs", [128, KI * C], BF16)
        sb("convacc0", [128, C], F32)
        sb("dtr_b", [DTR, C], BF16)
        sb("brow_bf", [DS, C], BF16)
        sb("crow_bf", [DS, C], BF16)
        for par in range(2):
            sb(f"b_bc{par}", [128, C], BF16)
            sb(f"c_bc{par}", [128, C], BF16)
        for par in range(2):
            sb(f"dt_f{par}", [128, C], F32)
            sb(f"dtx{par}", [128, C], BF16)
            sb(f"y_sb{par}", [128, C], F32)
        sb("u_sb", [128, C], F32)
        sb("hs_sb", [128, C], F32)
        sb("ys", [128, KI * C], BF16)
        sb("h_all", [128, KI * DS], F32)
        for par in range(2):
            sb(f"wout_slot{par}", [128, KI * 128], BF16)
        sb("out_acc", [128, KD * C], F32)
        for par in range(2):
            sb(f"wg_slot{par}", [128, KD * 256], BF16)
            sb(f"wu_slot{par}", [128, KD * 256], BF16)
            sb(f"gs_sb{par}", [128, C], BF16)
        for par in range(4):
            sb(f"wd_slot{par}", [128, D], BF16)
        sb("h_half", [128, 14 * C], BF16)
        for j in range(8):
            self.P.append(nc.alloc_psum_tensor(f"ps{j}", [128, C], F32))

    def mm_group(self, mms, reads, writes):
        def fn(e):
            last = None
            for out, lhsT, rhs, st, sp in mms:
                last = e.matmul(out, lhsT, rhs, start=st, stop=sp)
            return last
        self.p.op("tensor", fn, reads=reads, writes=writes)

    def xcol(self, name, i, w=C):
        return self.S[name][:, i * w:(i + 1) * w]

    def prologue(self):
        p, S = self.p, self.S

        wi4_dst = self.wi_bf.rearrange("(g p) (k m) -> g p k m", p=128, m=128)
        wi4_src = self.w_inT.rearrange("(k p) (g m) -> g p k m", p=128, m=128)
        for g in range(32):
            p.op("qgpsimd", lambda e, g=g: [e.dma_start(out=wi4_dst[g], in_=wi4_src[g])],
                 writes=("hbm_wi",), chan="wi")
        wo4_dst = self.wo_bf.rearrange("(m p) (k c) -> m p k c", p=128, c=128)
        wo4_src = self.w_outT.rearrange("(k p) (m c) -> m p k c", p=128, c=128)
        for mm_ in range(KD):
            p.op("qgpsimd", lambda e, mm_=mm_: [e.dma_start(out=wo4_dst[mm_], in_=wo4_src[mm_])],
                 writes=("hbm_wo",), chan="wo")
        for e_ in range(E):
            g_dst = self.wg_bf[e_].rearrange("(s p) (k m) -> s p k m", p=128, m=256)
            g_src = self.wg_in[e_].rearrange("(k p) (s m) -> s p k m", p=128, m=256)
            u_dst = self.wu_bf[e_].rearrange("(s p) (k m) -> s p k m", p=128, m=256)
            u_src = self.wu_in[e_].rearrange("(k p) (s m) -> s p k m", p=128, m=256)
            for s in range(14):
                p.op("qgpsimd", lambda e, s=s, g_dst=g_dst, g_src=g_src: [
                    e.dma_start(out=g_dst[s], in_=g_src[s])],
                    writes=(f"hbm_cg{e_}",), chan=f"cg{e_}")
                p.op("qgpsimd", lambda e, s=s, u_dst=u_dst, u_src=u_src: [
                    e.dma_start(out=u_dst[s], in_=u_src[s])],
                    writes=(f"hbm_cu{e_}",), chan=f"cu{e_}")
            p.op("qgpsimd", lambda e, e_=e_: [e.dma_start(out=self.wd_bf[e_][:], in_=self.wd_in[e_][:])],
                 writes=(f"hbm_cd{e_}",), chan=f"cd{e_}")

        loads = [("ident", self.ident_in), ("sel", self.sel_in), ("conv_w", self.conv_w_l),
                 ("conv_b", self.conv_b_l), ("Dvec", self.Dvec_l), ("bdt", self.bdt_l),
                 ("alog_sb", self.alog)]
        for nm, src in loads:
            p.op("qsync", lambda e, nm=nm, src=src: [e.dma_start(out=S[nm][:], in_=src[:])],
                 writes=(nm,), chan="cst_" + nm)
        p.op("qsync", lambda e: [e.dma_start(out=S["out_acc"][:, 0:KI * 96], in_=self.w_xT_l[:])],
             writes=(("oa", 0), ("oa", 1), ("oa", 2)), chan="cst_wx")
        p.op("qsync", lambda e: [e.dma_start(out=S["out_acc"][0:DTR, KI * 96:KI * 96 + DI], in_=self.w_dtT[:])],
             writes=(("oa", 3), ("oa", 4), ("oa", 5), ("oa", 6)), chan="cst_wdt")
        p.op("qsync", lambda e: [e.dma_start(out=S["out_acc"][:, 3648:3648 + KD * E], in_=self.rw_l[:])],
             writes=(("oa", 7),), chan="cst_rw")
        p.op("vector", lambda e: e.tensor_copy(S["w_x_sb"][:], S["out_acc"][:, 0:KI * 96]),
             reads=(("oa", 0), ("oa", 1), ("oa", 2)), writes=("w_x_sb",))
        p.op("vector", lambda e: e.tensor_copy(S["w_dt_sb"][:], S["out_acc"][0:DTR, KI * 96:KI * 96 + DI]),
             reads=(("oa", 3), ("oa", 4), ("oa", 5), ("oa", 6)), writes=("w_dt_sb",))
        p.op("vector", lambda e: e.tensor_copy(S["rw_sb"][:], S["out_acc"][:, 3648:3648 + KD * E]),
             reads=(("oa", 7),), writes=("rw_sb",))

        p.op("vector", lambda e: e.memset(S["ones_row_f"][:], 1.0), writes=("ones_row_f",))
        p.op("vector", lambda e: e.memset(S["ones_col_bf"][:], 1.0), writes=("ones_col_bf",))
        p.op("vector", lambda e: e.memset(S["cconst"][:, 0:1], EPS), writes=("cconst",))
        p.op("vector", lambda e: e.memset(S["cconst"][:, 1:2], 1.0), writes=("cconst",))
        p.op("vector", lambda e: e.memset(S["cconst"][:, 2:3], -0.5), writes=("cconst",))
        p.op("vector", lambda e: e.memset(S["mpad1"][:], 1e30), writes=("mpad1",))
        p.op("vector", lambda e: e.memset(S["mpad2"][:], 1e30), writes=("mpad2",))
        p.op("gpsimd", lambda e: e.memset(S["h_all"][:], 0.0),
             writes=tuple(("h", i, s) for i in range(KI) for s in range(DS)))
        p.op("gpsimd", lambda e: e.memset(S["xr"][:], 0.0),
             writes=tuple(("xr", i) for i in range(KI)))

        p.op("scalar", lambda e: e.activation(S["alog_sb"][:], S["alog_sb"][:], AF.Exp),
             reads=("alog_sb",), writes=("alog_sb",))
        p.op("tensor", lambda e: e.matmul(self.P[0][:, 0:DS], S["ones_row_f"][:], S["alog_sb"][:],
                                          start=True, stop=True),
             reads=("ones_row_f", "alog_sb"), writes=("ps0",))
        p.op("vector", lambda e: e.tensor_scalar(S["a_bc"][:], self.P[0][:, 0:DS], -1.0, None, ALU.mult),
             reads=("ps0",), writes=("a_bc",))

    def ln_phase(self, n):
        p, S, P = self.p, self.S, self.P
        t0 = n * C
        xnb = f"xnb{n % 2}"

        p.op("qsync", lambda e: [e.dma_start(
            out=S["out_acc"][:],
            in_=self.x_til[n * 128:(n + 1) * 128, :])],
            writes=tuple(("oa", k) for k in range(KD)), chan="x")

        for k in range(KD):
            xbp = "xb0"
            p.op("vector", lambda e, k=k, xbp=xbp: e.tensor_copy(S[xbp][:], self.xcol("out_acc", k)),
                 reads=(("oa", k),), writes=(xbp,))
            self.mm_group([(P[0][0:1, :], S["ones_col_bf"][:], S[xbp][:], k == 0, k == KD - 1)],
                          reads=(xbp, "ones_col_bf"), writes=("ps0",))
            p.op("scalar", lambda e, xbp=xbp: e.activation(S[xbp][:], S[xbp][:], AF.Square),
                 reads=(xbp,), writes=(xbp,))
            self.mm_group([(P[1][0:1, :], S["ones_col_bf"][:], S[xbp][:], k == 0, k == KD - 1)],
                          reads=(xbp, "ones_col_bf"), writes=("ps1",))
        p.op("vector", lambda e: e.tensor_scalar(S["m_row"][:], P[0][0:1, :], 1.0 / D, None, ALU.mult),
             reads=("ps0",), writes=("m_row",))
        p.op("vector", lambda e: e.tensor_mul(S["var_row"][:], S["m_row"][:], S["m_row"][:]),
             reads=("m_row",), writes=("var_row",))
        p.op("vector", lambda e: e.scalar_tensor_tensor(S["var_row"][:], P[1][0:1, :], 1.0 / D,
                                                        S["var_row"][:], ALU.mult, ALU.subtract),
             reads=("ps1", "var_row"), writes=("var_row",))
        p.op("scalar", lambda e: e.activation(S["rs_row"][:], S["var_row"][:], AF.Ln,
                                              bias=S["cconst"][0:1, 0:1]),
             reads=("var_row", "cconst"), writes=("rs_row",))
        p.op("scalar", lambda e: e.activation(S["rs_row"][:], S["rs_row"][:], AF.Exp,
                                              scale=S["cconst"][0:1, 2:3]),
             reads=("rs_row", "cconst"), writes=("rs_row",))
        p.op("vector", lambda e: e.scalar_tensor_tensor(S["mrs_row"][:], S["m_row"][:], -1.0,
                                                        S["rs_row"][:], ALU.mult, ALU.mult),
             reads=("m_row", "rs_row"), writes=("mrs_row",))
        p.op("tensor", lambda e: e.matmul(P[2][:], S["ones_row_f"][:], S["rs_row"][:], start=True, stop=True),
             reads=("ones_row_f", "rs_row"), writes=("ps2",))
        p.op("tensor", lambda e: e.matmul(P[3][:], S["ones_row_f"][:], S["mrs_row"][:], start=True, stop=True),
             reads=("ones_row_f", "mrs_row"), writes=("ps3",))
        for k in range(KD):
            p.op("vector", lambda e, k=k: e.tensor_tensor(S["lntmp"][:], self.xcol("out_acc", k),
                                                          P[2][:], ALU.mult),
                 reads=(("oa", k), "ps2"), writes=("lntmp",))
            p.op("vector", lambda e, k=k: e.tensor_tensor(self.xcol(xnb, k), S["lntmp"][:],
                                                          P[3][:], ALU.add),
                 reads=("lntmp", "ps3"), writes=((xnb, k),))

    def router_phase(self, n):
        p, S, P = self.p, self.S, self.P
        xnb = f"xnb{n % 2}"
        self.mm_group([(P[4][0:E, :], S["rw_sb"][:, k * E:(k + 1) * E], self.xcol(xnb, k),
                        k == 0, k == KD - 1) for k in range(KD)],
                      reads=tuple((xnb, k) for k in range(KD)) + ("rw_sb",),
                      writes=("ps4",))
        p.op("scalar", lambda e: e.copy(S["lntmp"][0:E, :], P[4][0:E, :]), reads=("ps4",), writes=("lntmp",))
        for b in range(4):
            p.op("tensor", lambda e, b=b: e.transpose(P[5][:, 8 * b:8 * b + 8],
                                                      S["lntmp"][0:E, 128 * b:128 * (b + 1)],
                                                      S["ident"][0:E, 0:E]),
                 reads=("lntmp", "ident"), writes=("ps5",))
        p.op("vector", lambda e: e.tensor_copy(S["ltok"][:], P[5][:, 0:4 * E]),
             reads=("ps5",), writes=("ltok",))
        for b in range(4):
            sl = slice(8 * b, 8 * b + 8)
            cb = slice(b, b + 1)
            p.op("vector", lambda e, sl=sl, cb=cb: e.reduce_max(S["rm1"][:, cb], S["ltok"][:, sl], axis=AX.X),
                 reads=("ltok",), writes=("rm1",))
            p.op("vector", lambda e, cb=cb: e.tensor_copy(S["mpad1"][:, 0:1], S["rm1"][:, cb]),
                 reads=("rm1",), writes=("mpad1",))
            p.op("vector", lambda e, sl=sl: e.match_replace(S["l2t"][:, sl], S["mpad1"][:], S["ltok"][:, sl], -1e30),
                 reads=("mpad1", "ltok"), writes=("l2t",))
            p.op("vector", lambda e, sl=sl, cb=cb: e.reduce_max(S["rm2"][:, cb], S["l2t"][:, sl], axis=AX.X),
                 reads=("l2t",), writes=("rm2",))
            p.op("vector", lambda e, cb=cb: e.tensor_copy(S["mpad2"][:, 0:1], S["rm2"][:, cb]),
                 reads=("rm2",), writes=("mpad2",))
            p.op("vector", lambda e, sl=sl: e.match_replace(S["l3t"][:, sl], S["mpad2"][:], S["l2t"][:, sl], -1e30),
                 reads=("mpad2", "l2t"), writes=("l3t",))
        p.op("vector", lambda e: e.tensor_sub(S["rd_"][:], S["rm2"][:], S["rm1"][:]),
             reads=("rm2", "rm1"), writes=("rd_",))
        p.op("scalar", lambda e: e.activation(S["re_"][:], S["rd_"][:], AF.Exp),
             reads=("rd_",), writes=("re_",))
        p.op("vector", lambda e: e.tensor_scalar(S["rw1"][:], S["re_"][:], 1.0, None, ALU.add),
             reads=("re_",), writes=("rw1",))
        p.op("vector", lambda e: e.reciprocal(S["rw1"][:], S["rw1"][:]),
             reads=("rw1",), writes=("rw1",))
        p.op("vector", lambda e: e.tensor_scalar(S["rw2"][:], S["rw1"][:], -1.0, 1.0, ALU.mult, ALU.add),
             reads=("rw1",), writes=("rw2",))
        p.op("vector", lambda e: e.tensor_tensor(S["eq1"][:], S["ltok"][:], S["l2t"][:], ALU.not_equal),
             reads=("ltok", "l2t"), writes=("eq1",))
        p.op("vector", lambda e: e.tensor_tensor(S["eq2"][:], S["l2t"][:], S["l3t"][:], ALU.not_equal),
             reads=("l2t", "l3t"), writes=("eq2",))
        for b in range(4):
            sl = slice(8 * b, 8 * b + 8)
            cb = slice(b, b + 1)
            p.op("vector", lambda e, sl=sl, cb=cb: e.tensor_scalar(S["eq1"][:, sl], S["eq1"][:, sl],
                                                                   S["rw1"][:, cb], None, ALU.mult),
                 reads=("eq1", "rw1"), writes=("eq1",))
            p.op("vector", lambda e, sl=sl, cb=cb: e.tensor_scalar(S["eq2"][:, sl], S["eq2"][:, sl],
                                                                   S["rw2"][:, cb], None, ALU.mult),
                 reads=("eq2", "rw2"), writes=("eq2",))
        p.op("vector", lambda e: e.tensor_add(S["cmb_tok"][:], S["eq1"][:], S["eq2"][:]),
             reads=("eq1", "eq2"), writes=("cmb_tok",))
        for b in range(4):
            p.op("tensor", lambda e, b=b: e.transpose(P[6][0:E, 128 * b:128 * (b + 1)],
                                                      S["cmb_tok"][:, 8 * b:8 * b + 8],
                                                      S["ident"][:]),
                 reads=("cmb_tok", "ident"), writes=("ps6",))
        p.op("scalar", lambda e: e.copy(S["cmb_rows_bf"][:], P[6][0:E, :]),
             reads=("ps6",), writes=("cmb_rows_bf",))

    def front_phase(self, n):
        p, S, P = self.p, self.S, self.P
        xnb = f"xnb{n % 2}"
        if n > 0:
            for i in range(KI):
                o = i * (C + 3)
                p.op("gpsimd", lambda e, o=o: e.tensor_copy(S["xr"][:, o:o + 3], S["xr"][:, o + C:o + C + 3]),
                     reads=(("xr", i),), writes=(("xr", i),))
        for mt in range(32):
            par = mt % 2
            bank = mt % 4
            p.op("qsync", lambda e, mt=mt, par=par: [e.dma_start(
                out=S[f"win_slot{par}"][:],
                in_=self.wi_bf[mt * 128:(mt + 1) * 128, :])],
                reads=("hbm_wi",), writes=(("win", par),), chan=f"win{par}")
            self.mm_group([(P[bank][:], S[f"win_slot{par}"][:, k * 128:(k + 1) * 128],
                            self.xcol(xnb, k), k == 0, k == KD - 1) for k in range(KD)],
                          reads=tuple((xnb, k) for k in range(KD)) + (("win", par),),
                          writes=(f"ps{bank}",))
            if mt < KI:
                o = mt * (C + 3)
                p.op("scalar", lambda e, mt=mt, bank=bank, o=o: e.copy(S["xr"][:, o + 3:o + 3 + C], P[bank][:]),
                     reads=(f"ps{bank}",), writes=(("xr", mt),))
            else:
                zt = mt - KI
                p.op("scalar", lambda e, zt=zt, bank=bank: e.activation(self.xcol("zs", zt), P[bank][:], AF.Silu),
                     reads=(f"ps{bank}",), writes=(("zs", zt),))
    def conv_unit(self, i):
        p, S = self.p, self.S
        o = i * (C + 3)
        ca = "convacc0"
        p.op("vector", lambda e, o=o, i=i, ca=ca: e.tensor_scalar(
            S[ca][:], S["xr"][:, o:o + C], S["conv_w"][:, i * DC:i * DC + 1], None, ALU.mult),
            reads=(("xr", i), "conv_w"), writes=(ca,))
        for kk in range(1, DC):
            p.op("vector", lambda e, o=o, i=i, kk=kk, ca=ca: e.scalar_tensor_tensor(
                S[ca][:], S["xr"][:, o + kk:o + kk + C], S["conv_w"][:, i * DC + kk:i * DC + kk + 1],
                S[ca][:], ALU.mult, ALU.add),
                reads=(("xr", i), "conv_w", ca), writes=(ca,))
        p.op("scalar", lambda e, i=i, ca=ca: e.activation(self.xcol("xs", i), S[ca][:], AF.Silu,
                                                          bias=S["conv_b"][:, i:i + 1]),
             reads=(ca, "conv_b"), writes=(("xs", i),))

    def xdbl_unit(self):
        p, S, P = self.p, self.S, self.P
        xs_reads = tuple(("xs", i) for i in range(KI))
        self.mm_group([(P[4][0:DTR, :], S["w_x_sb"][:, k * 96:k * 96 + DTR], self.xcol("xs", k),
                        k == 0, k == KI - 1) for k in range(KI)],
                      reads=xs_reads + ("w_x_sb",), writes=("ps4",))
        self.mm_group([(P[5][0:DS, :], S["w_x_sb"][:, k * 96 + DTR:k * 96 + DTR + DS], self.xcol("xs", k),
                        k == 0, k == KI - 1) for k in range(KI)],
                      reads=xs_reads + ("w_x_sb",), writes=("ps5",))
        self.mm_group([(P[5][32:32 + DS, :], S["w_x_sb"][:, k * 96 + 80:k * 96 + 96], self.xcol("xs", k),
                        k == 0, k == KI - 1) for k in range(KI)],
                      reads=xs_reads + ("w_x_sb",), writes=("ps5",))
        p.op("scalar", lambda e: e.copy(S["dtr_b"][:], P[4][0:DTR, :]), reads=("ps4",), writes=("dtr_b",))
        p.op("scalar", lambda e: e.copy(S["brow_bf"][:], P[5][0:DS, :]), reads=("ps5",), writes=("brow_bf",))
        p.op("scalar", lambda e: e.copy(S["crow_bf"][:], P[5][32:32 + DS, :]), reads=("ps5",), writes=("crow_bf",))

    def scan_dt_unit(self, pr):
        p, S, P = self.p, self.S, self.P
        for i in (2 * pr, 2 * pr + 1):
            par = i % 2
            p.op("tensor", lambda e, i=i: e.matmul(P[6][:], S["w_dt_sb"][:, 128 * i:128 * (i + 1)],
                                                   S["dtr_b"][:], start=True, stop=True),
                 reads=("w_dt_sb", "dtr_b"), writes=("ps6",))
            p.op("scalar", lambda e, i=i, par=par: e.activation(S[f"dt_f{par}"][:], P[6][:], AF.Exp,
                                                                bias=S["bdt"][:, i:i + 1]),
                 reads=("ps6", "bdt"), writes=(f"dt_f{par}",))
            p.op("scalar", lambda e, par=par: e.activation(S[f"dt_f{par}"][:], S[f"dt_f{par}"][:], AF.Ln,
                                                           bias=S["cconst"][:, 1:2]),
                 reads=(f"dt_f{par}", "cconst"), writes=(f"dt_f{par}",))
            p.op("vector", lambda e, i=i, par=par: e.tensor_tensor(S[f"dtx{par}"][:], S[f"dt_f{par}"][:],
                                                                   self.xcol("xs", i), ALU.mult),
                 reads=(f"dt_f{par}", ("xs", i)), writes=(f"dtx{par}",))
            p.op("vector", lambda e, i=i, par=par: e.tensor_scalar(S[f"y_sb{par}"][:], self.xcol("xs", i),
                                                                   S["Dvec"][:, i:i + 1], None, ALU.mult),
                 reads=(("xs", i), "Dvec"), writes=(f"y_sb{par}",))

    def scan_s_unit(self, pr, s):
        p, S, P = self.p, self.S, self.P
        sp = s % 2
        p.op("tensor", lambda e, s=s: e.matmul(P[5][:], S["sel"][:, 128 * s:128 * (s + 1)],
                                               S["brow_bf"][:], start=True, stop=True),
             reads=("sel", "brow_bf"), writes=("ps5",))
        p.op("scalar", lambda e, sp=sp: e.copy(S[f"b_bc{sp}"][:], P[5][:]),
             reads=("ps5",), writes=(f"b_bc{sp}",))
        p.op("tensor", lambda e, s=s: e.matmul(P[6][:], S["sel"][:, 128 * s:128 * (s + 1)],
                                               S["crow_bf"][:], start=True, stop=True),
             reads=("sel", "crow_bf"), writes=("ps6",))
        p.op("scalar", lambda e, sp=sp: e.copy(S[f"c_bc{sp}"][:], P[6][:]),
             reads=("ps6",), writes=(f"c_bc{sp}",))
        for i in (2 * pr, 2 * pr + 1):
            par = i % 2
            hkey = ("h", i, s)
            hcol = i * DS + s
            ab = P[4] if par == 0 else P[7]
            abk = "ps4" if par == 0 else "ps7"
            p.op("scalar", lambda e, par=par, s=s, ab=ab: e.activation(ab[:], S[f"dt_f{par}"][:], AF.Exp,
                                                                       scale=S["a_bc"][:, s:s + 1]),
                 reads=(f"dt_f{par}", "a_bc"), writes=(abk,))
            p.op("vector", lambda e, par=par, sp=sp: e.tensor_tensor(S["u_sb"][:], S[f"dtx{par}"][:],
                                                                     S[f"b_bc{sp}"][:], ALU.mult),
                 reads=(f"dtx{par}", f"b_bc{sp}"), writes=("u_sb",))
            p.op("vector", lambda e, par=par, hcol=hcol, ab=ab: e.tensor_tensor_scan(
                S["hs_sb"][:], ab[:], S["u_sb"][:], S["h_all"][:, hcol:hcol + 1],
                ALU.mult, ALU.add),
                reads=(abk, "u_sb", hkey), writes=("hs_sb",))
            p.op("gpsimd", lambda e, hcol=hcol: e.tensor_copy(S["h_all"][:, hcol:hcol + 1], S["hs_sb"][:, C - 1:C]),
                 reads=("hs_sb",), writes=(hkey,))
            p.op("vector", lambda e, sp=sp: e.tensor_tensor(S["u_sb"][:], S["hs_sb"][:],
                                                            S[f"c_bc{sp}"][:], ALU.mult),
                 reads=("hs_sb", f"c_bc{sp}"), writes=("u_sb",))
            p.op("vector", lambda e, par=par: e.tensor_tensor(S[f"y_sb{par}"][:], S[f"y_sb{par}"][:],
                                                              S["u_sb"][:], ALU.add),
                 reads=(f"y_sb{par}", "u_sb"), writes=(f"y_sb{par}",))

    def scan_ys_unit(self, pr):
        p, S = self.p, self.S
        for i in (2 * pr, 2 * pr + 1):
            par = i % 2
            p.op("vector", lambda e, i=i, par=par: e.tensor_tensor(self.xcol("ys", i), S[f"y_sb{par}"][:],
                                                                   self.xcol("zs", i), ALU.mult),
                 reads=(f"y_sb{par}", ("zs", i)), writes=(("ys", i),))

    def moe_pre_unit(self, n, e_):
        p, S, P = self.p, self.S, self.P
        xnb = f"xnb{n % 2}"
        par = e_ % 2
        p.op("tensor", lambda e, e_=e_: e.matmul(P[5][:], S["sel"][0:E, 128 * e_:128 * (e_ + 1)],
                                                 S["cmb_rows_bf"][:], start=True, stop=True),
             reads=("sel", "cmb_rows_bf"), writes=("ps5",))
        p.op("scalar", lambda e, par=par: e.copy(S[f"cmb_sb{par}"][:], P[5][:]),
             reads=("ps5",), writes=(f"cmb_sb{par}",))
        for k in range(KD):
            p.op("vector", lambda e, k=k, par=par, xnb=xnb: e.tensor_tensor(
                self.xcol("xnb_u", k), self.xcol(xnb, k), S[f"cmb_sb{par}"][:], ALU.mult),
                reads=((xnb, k), f"cmb_sb{par}"), writes=(("xnu", k),))

    def moe_gu_unit(self, n, e_, h, st):
        p, S, P = self.p, self.S, self.P
        xnb = f"xnb{n % 2}"
        par = st % 2
        sidx = h * 7 + st
        p.op("qsync", lambda e, sidx=sidx, par=par, e_=e_: [e.dma_start(
            out=S[f"wg_slot{par}"][:],
            in_=self.wg_bf[e_][sidx * 128:(sidx + 1) * 128, :])],
            reads=("hbm_cg%d" % e_,), writes=(("wgs", par),), chan=f"wg{par}")
        p.op("qsync", lambda e, sidx=sidx, par=par, e_=e_: [e.dma_start(
            out=S[f"wu_slot{par}"][:],
            in_=self.wu_bf[e_][sidx * 128:(sidx + 1) * 128, :])],
            reads=("hbm_cu%d" % e_,), writes=(("wus", par),), chan=f"wu{par}")
        for j in range(2):
            ftl = st * 2 + j
            self.mm_group([(P[j][:], S[f"wg_slot{par}"][:, k * 256 + 128 * j:k * 256 + 128 * (j + 1)],
                            self.xcol(xnb, k), k == 0, k == KD - 1) for k in range(KD)],
                          reads=tuple((xnb, k) for k in range(KD)) + (("wgs", par),),
                          writes=(f"ps{j}",))
            self.mm_group([(P[2 + j][:], S[f"wu_slot{par}"][:, k * 256 + 128 * j:k * 256 + 128 * (j + 1)],
                            self.xcol("xnb_u", k), k == 0, k == KD - 1) for k in range(KD)],
                          reads=tuple(("xnu", k) for k in range(KD)) + (("wus", par),),
                          writes=(f"ps{2 + j}",))
            p.op("scalar", lambda e, j=j: e.activation(S[f"gs_sb{j}"][:], P[j][:], AF.Silu),
                 reads=(f"ps{j}",), writes=(f"gs_sb{j}",))
            p.op("vector", lambda e, j=j, ftl=ftl: e.tensor_tensor(self.xcol("h_half", ftl), S[f"gs_sb{j}"][:],
                                                                   P[2 + j][:], ALU.mult),
                 reads=(f"gs_sb{j}", f"ps{2 + j}"), writes=(("hh", ftl),))

    def moe_down_units(self, e_, h):
        p, S, P = self.p, self.S, self.P
        units = []
        for ks in range(14):
            def unit(ks=ks, e_=e_, h=h):
                par = ks % 4
                s0 = h * 14 + ks
                p.op("qsync", lambda e, s0=s0, par=par, e_=e_: [e.dma_start(
                    out=S[f"wd_slot{par}"][:],
                    in_=self.wd_bf[e_][128 * s0:128 * (s0 + 1), :])],
                    reads=("hbm_cd%d" % e_,), writes=(("wds", par),), chan=f"wd{par}")
                for grp in range(2):
                    mlist = range(4 * grp, 4 * grp + 4)
                    self.mm_group(
                        [(P[m][:], S[f"wd_slot{par}"][:, 128 * m:128 * (m + 1)],
                          self.xcol("h_half", ks), ks == 0, ks == 13)
                         for m in mlist],
                        reads=(("wds", par), ("hh", ks)),
                        writes=tuple(f"ps{m}" for m in mlist))
            units.append(unit)

        def evac(e_=e_, h=h):
            for m in range(8):
                p.op("vector", lambda e, m=m: e.tensor_tensor(self.xcol("out_acc", m), self.xcol("out_acc", m),
                                                              P[m][:], ALU.add),
                     reads=(("oa", m), f"ps{m}"), writes=(("oa", m),))
        units.append(evac)
        return units

    def wout_phase(self, n):
        p, S, P = self.p, self.S, self.P
        for m in range(KD):
            par = m % 2
            p.op("qsync", lambda e, m=m, par=par: [e.dma_start(
                out=S[f"wout_slot{par}"][:],
                in_=self.wo_bf[m * 128:(m + 1) * 128, :])],
                reads=("hbm_wo",), writes=(("wos", par),), chan=f"wout{par}")
            self.mm_group([(P[par][:], S[f"wout_slot{par}"][:, k * 128:(k + 1) * 128],
                            self.xcol("ys", k), k == 0, k == KI - 1) for k in range(KI)],
                          reads=tuple(("ys", k) for k in range(KI)) + (("wos", par),),
                          writes=(f"ps{par}",))
            p.op("vector", lambda e, m=m, par=par: e.tensor_tensor(self.xcol("out_acc", m),
                                                                   self.xcol("out_acc", m), P[par][:], ALU.add),
                 reads=(("oa", m), f"ps{par}"), writes=(("oa", m),))

    def store_phase(self, n):
        p, S = self.p, self.S
        p.op("qsync", lambda e: [e.dma_start(
            out=self.out_til[n * 128:(n + 1) * 128, :],
            in_=S["out_acc"][:])],
            reads=tuple(("oa", k) for k in range(KD)), writes=("stored",), chan="st")

    def chunk(self, n):
        self.ln_phase(n)
        self.router_phase(n)
        self.front_phase(n)
        mamba_units = [lambda i=i: self.conv_unit(i) for i in range(KI)]
        mamba_units.append(lambda: self.xdbl_unit())
        for pr in range(E):
            mamba_units.append(lambda pr=pr: self.scan_dt_unit(pr))
            for s in range(DS):
                mamba_units.append(lambda pr=pr, s=s: self.scan_s_unit(pr, s))
            mamba_units.append(lambda pr=pr: self.scan_ys_unit(pr))
        moe_units = []
        for e_ in range(E):
            moe_units.append(lambda e_=e_: self.moe_pre_unit(n, e_))
            for h in range(2):
                for st in range(7):
                    moe_units.append(lambda e_=e_, h=h, st=st: self.moe_gu_unit(n, e_, h, st))

                def down_block(e_=e_, h=h):
                    for u in self.moe_down_units(e_, h):
                        u()
                moe_units.append(down_block)
        na, nb = len(mamba_units), len(moe_units)
        ia = ib = 0
        while ia < na or ib < nb:
            if ib * na <= ia * nb and ib < nb:
                moe_units[ib]()
                ib += 1
            elif ia < na:
                mamba_units[ia]()
                ia += 1
            else:
                moe_units[ib]()
                ib += 1
        self.wout_phase(n)
        self.store_phase(n)

    def build(self):
        self.prologue()
        for n in range(self.NCH):
            self.chunk(n)
        self.p.final_wait("sync", "st", self.p.barrier_val("st"))
        self.p.render()



def _host_consts():
    ident = np.eye(128, dtype=np.float32)
    sel = np.zeros((DS, DS * 128), dtype=np.float32)
    for s in range(DS):
        sel[s, 128 * s:128 * (s + 1)] = 1.0
    sel = sel.astype(ml_dtypes.bfloat16)
    return ident, sel


def _host_prep(inputs, T):
    W_in = inputs["L0_attn_W_in"]
    conv_w = inputs["L0_attn_conv_w"]
    conv_b = inputs["L0_attn_conv_b"]
    A_log = inputs["L0_attn_A_log"]
    Dv = inputs["L0_attn_D"]
    W_x = inputs["L0_attn_W_x"]
    W_dt = inputs["L0_attn_W_dt"]
    b_dt = inputs["L0_attn_b_dt"]
    W_out = inputs["L0_attn_W_out"]
    router_w = inputs["L0_ffn_router_w"]
    wg = inputs["L0_ffn_w_gate"]
    wu = inputs["L0_ffn_w_up"]
    wd = inputs["L0_ffn_w_down"]
    ident, sel = _host_consts()

    f32 = np.float32
    shared = {
        "w_inT": np.ascontiguousarray(W_in.T, dtype=f32),
        "w_outT": np.ascontiguousarray(W_out.T, dtype=f32),
        "w_xT_l": np.ascontiguousarray(
            W_x.T.reshape(KI, 128, 96).transpose(1, 0, 2).reshape(128, KI * 96), dtype=f32),
        "w_dtT": np.ascontiguousarray(W_dt.T, dtype=f32),
        "rw_l": np.ascontiguousarray(
            router_w.T.reshape(KD, 128, E).transpose(1, 0, 2).reshape(128, KD * E), dtype=f32),
        "conv_w_l": np.ascontiguousarray(
            conv_w.reshape(KI, 128, DC).transpose(1, 0, 2).reshape(128, KI * DC), dtype=f32),
        "conv_b_l": np.ascontiguousarray(conv_b.reshape(KI, 128).T, dtype=f32),
        "Dvec_l": np.ascontiguousarray(Dv.reshape(KI, 128).T, dtype=f32),
        "bdt_l": np.ascontiguousarray(b_dt.reshape(KI, 128).T, dtype=f32),
        "alog": np.ascontiguousarray(A_log.reshape(1, DS), dtype=f32),
        "ident_in": ident,
        "sel_in": sel,
    }
    for e in range(E):
        shared[f"wg{e}"] = np.ascontiguousarray(wg[e], dtype=f32)
        shared[f"wu{e}"] = np.ascontiguousarray(wu[e], dtype=f32)
        shared[f"wd{e}"] = np.ascontiguousarray(wd[e], dtype=f32)
    return shared


def kernel(**inputs):
    x = np.asarray(inputs["L0_attn_x"])
    B, T, _ = x.shape
    assert B == 8 and T % C == 0

    nc = bass.Bass("TRN2", target_bir_lowering=False)
    mk = MK(nc, T)
    mk.build()

    shared = _host_prep(inputs, T)
    nch = T // C
    in_maps = []
    for b in range(B):
        m = dict(shared)
        m["x_til"] = _tile_x(x[b], nch)
        in_maps.append(m)

    res = run_bass_kernel_spmd(nc, in_maps, core_ids=list(range(B)))
    out = np.stack([_untile_out(np.asarray(res.results[b]["out_til"]), nch, T)
                    for b in range(B)], axis=0)
    return np.ascontiguousarray(out, dtype=np.float32)


def _tile_x(xb, nch):
    xT = xb.T
    return np.ascontiguousarray(
        xT.reshape(KD, 128, nch, C).transpose(2, 1, 0, 3).reshape(nch * 128, KD * C))


def _untile_out(ot, nch, T):
    outT = ot.reshape(nch, 128, KD, C).transpose(2, 1, 0, 3).reshape(D, T)
    return outT.T


# version 04
# speedup vs baseline: 1.2067x; 1.0656x over previous
import numpy as np
import ml_dtypes
import concourse.bass as bass
import concourse.mybir as mybir
from concourse.bass_utils import run_bass_kernel_spmd

F32 = mybir.dt.float32
BF16 = mybir.dt.bfloat16
AF = mybir.ActivationFunctionType
ALU = mybir.AluOpType
AX = mybir.AxisListType


_QUEUES = {"qsync": "sync", "qscalar": "scalar", "qgpsimd": "gpsimd"}
_ENGINES = ("tensor", "vector", "scalar", "gpsimd", "sync")


class Prog:
    def __init__(self, nc):
        self.nc = nc
        self.sems = {}
        self.cnt = {}
        self.clock = {e: {} for e in _ENGINES}
        self.snap = {}
        self.last_w = {}
        self.readers = {}
        self.plan = {e: [] for e in _ENGINES}
        self.n_ops = 0
        self.n_waits = 0

    def _sem(self, name):
        if name not in self.sems:
            self.sems[name] = self.nc.alloc_semaphore(name="s_" + name)
            self.cnt[name] = 0
        return self.sems[name]

    def op(self, stream, fn, reads=(), writes=(), chan=None, n=1):
        is_dma = stream in _QUEUES
        if is_dma:
            issuer = _QUEUES[stream]
            assert chan is not None
            sem_name, delta = "c_" + chan, 16 * n
        else:
            issuer = stream
            sem_name, delta = stream, 1
        self._sem(sem_name)

        deps = {}

        def add_dep(d):
            if d and deps.get(d[0], 0) < d[1]:
                deps[d[0]] = d[1]

        for b in reads:
            add_dep(self.last_w.get(b))
        for b in writes:
            add_dep(self.last_w.get(b))
            for s, v in self.readers.get(b, {}).items():
                add_dep((s, v))

        clock = self.clock[issuer]
        waits = []
        for dep in sorted(deps.items()):
            if (not is_dma) and stream == "tensor" and dep[0] == "tensor":
                self._absorb(clock, dep)
                continue
            if clock.get(dep[0], 0) < dep[1]:
                waits.append(dep)
                self._absorb(clock, dep)

        new_cnt = self.cnt[sem_name] + delta
        self.cnt[sem_name] = new_cnt
        me = (sem_name, new_cnt)
        self.snap[me] = dict(clock)
        for b in writes:
            self.last_w[b] = me
            self.readers[b] = {}
        for b in reads:
            self.readers.setdefault(b, {})[sem_name] = new_cnt

        self.plan[issuer].append((waits, fn, (sem_name, 16 if is_dma else 1), is_dma))
        self.n_ops += 1
        self.n_waits += len(waits)
        return me

    def _absorb(self, clock, dep):
        sem, val = dep
        if clock.get(sem, 0) >= val:
            return
        clock[sem] = val
        snap = self.snap.get((sem, val))
        if snap:
            for s2, v2 in snap.items():
                if clock.get(s2, 0) < v2:
                    clock[s2] = v2

    def barrier_val(self, chan):
        name = "c_" + chan if ("c_" + chan) in self.cnt else chan
        return self.cnt.get(name, 0)

    def final_wait(self, on_engine, chan, val):
        name = "c_" + chan if ("c_" + chan) in self.cnt else chan
        self.plan[on_engine].append(([(name, val)], None, None, False))

    def render(self):
        nc = self.nc
        with nc.Block() as block:
            for eng_name in _ENGINES:
                ops = self.plan[eng_name]
                if not ops:
                    continue

                def make(ops=ops):
                    def body(eng):
                        for waits, fn, inc, is_dma in ops:
                            for (s, v) in waits:
                                eng.wait_ge(self.sems[s], v)
                            if fn is None:
                                continue
                            res = fn(eng)
                            if inc is None:
                                continue
                            if is_dma:
                                insts = res if isinstance(res, (list, tuple)) else [res]
                                for inst in insts:
                                    inst.then_inc(self.sems[inc[0]], 16)
                            else:
                                res.then_inc(self.sems[inc[0]], 1)
                    return body

                getattr(block, eng_name)(make())


D = 1024
DI = 2048
DS = 16
DTR = 64
DC = 4
E = 8
FF = 3584
KD = D // 128
KI = DI // 128
KF = FF // 128
C = 512
EPS = 1e-5


class MK:

    def __init__(self, nc, T):
        self.nc = nc
        self.T = T
        self.NCH = T // C
        self.p = Prog(nc)
        self.S = {}
        self.P = []
        self._declare_dram()
        self._alloc_sbuf()

    def _declare_dram(self):
        nc = self.nc
        dt = nc.dram_tensor
        self.w_inT = dt("w_inT", [D, 2 * DI], F32, kind="ExternalInput").ap()
        self.w_outT = dt("w_outT", [DI, D], F32, kind="ExternalInput").ap()
        self.w_xT_l = dt("w_xT_l", [128, KI * 96], F32, kind="ExternalInput").ap()
        self.w_dtT = dt("w_dtT", [DTR, DI], F32, kind="ExternalInput").ap()
        self.rw_l = dt("rw_l", [128, KD * E], F32, kind="ExternalInput").ap()
        self.conv_w_l = dt("conv_w_l", [128, KI * DC], F32, kind="ExternalInput").ap()
        self.conv_b_l = dt("conv_b_l", [128, KI], F32, kind="ExternalInput").ap()
        self.Dvec_l = dt("Dvec_l", [128, KI], F32, kind="ExternalInput").ap()
        self.bdt_l = dt("bdt_l", [128, KI], F32, kind="ExternalInput").ap()
        self.alog = dt("alog", [1, DS], F32, kind="ExternalInput").ap()
        self.ident_in = dt("ident_in", [128, 128], F32, kind="ExternalInput").ap()
        self.sel_in = dt("sel_in", [DS, DS * 128], BF16, kind="ExternalInput").ap()
        self.wg_in = [dt(f"wg{e}", [D, FF], F32, kind="ExternalInput").ap() for e in range(E)]
        self.wu_in = [dt(f"wu{e}", [D, FF], F32, kind="ExternalInput").ap() for e in range(E)]
        self.wd_in = [dt(f"wd{e}", [FF, D], F32, kind="ExternalInput").ap() for e in range(E)]
        dti = lambda n, s: dt(n, s, BF16, kind="Internal").ap()
        self.wi_bf = dti("wi_bf", [32 * 128, KD * 128])
        self.wo_bf = dti("wo_bf", [KD * 128, KI * 128])
        self.wg_bf = [dti(f"wg_bf{e}", [14 * 128, KD * 256]) for e in range(E)]
        self.wu_bf = [dti(f"wu_bf{e}", [14 * 128, KD * 256]) for e in range(E)]
        self.wd_bf = [dti(f"wd_bf{e}", [FF, D]) for e in range(E)]
        self.out_til = dt("out_til", [self.NCH * 128, KD * C], F32, kind="ExternalOutput").ap()
        self.x_til = dt("x_til", [self.NCH * 128, KD * C], F32, kind="ExternalInput").ap()

    def _alloc_sbuf(self):
        nc = self.nc
        S = self.S

        def sb(name, shape, dtype):
            S[name] = nc.alloc_sbuf_tensor(name, shape, dtype)

        sb("ones_row_f", [1, 128], F32)
        sb("ones_col_bf", [128, 1], BF16)
        sb("ident", [128, 128], F32)
        sb("sel", [DS, DS * 128], BF16)
        sb("conv_w", [128, KI * DC], F32)
        sb("conv_b", [128, KI], F32)
        sb("Dvec", [128, KI], F32)
        sb("bdt", [128, KI], F32)
        sb("alog_sb", [1, DS], F32)
        sb("a_bc", [128, DS], F32)
        sb("cconst", [128, 3], F32)
        sb("mpad1", [128, 8], F32)
        sb("mpad2", [128, 8], F32)
        sb("w_x_sb", [128, KI * 96], BF16)
        sb("w_dt_sb", [DTR, DI], BF16)
        sb("rw_sb", [128, KD * E], BF16)
        sb("xb0", [128, C], BF16)
        for par in range(2):
            sb(f"xnb{par}", [128, KD * C], BF16)
        sb("m_row", [1, C], F32)
        sb("var_row", [1, C], F32)
        sb("rs_row", [1, C], F32)
        sb("mrs_row", [1, C], F32)
        sb("lntmp", [128, C], F32)
        sb("ltok", [128, 4 * E], F32)
        sb("l2t", [128, 4 * E], F32)
        sb("l3t", [128, 4 * E], F32)
        sb("eq1", [128, 4 * E], F32)
        sb("eq2", [128, 4 * E], F32)
        sb("cmb_tok", [128, 4 * E], F32)
        sb("rm1", [128, 4], F32)
        sb("rm2", [128, 4], F32)
        sb("rd_", [128, 4], F32)
        sb("re_", [128, 4], F32)
        sb("rw1", [128, 4], F32)
        sb("rw2", [128, 4], F32)
        sb("cmb_rows_bf", [E, C], BF16)
        for par in range(2):
            sb(f"cmb_sb{par}", [128, C], BF16)
        for par in range(2):
            sb(f"win_slot{par}", [128, KD * 128], BF16)
        sb("xr", [128, KI * (C + 3)], BF16)
        sb("zs", [128, KI * C], BF16)
        sb("xs", [128, KI * C], BF16)
        sb("convacc0", [128, C], F32)
        sb("dtr_b", [DTR, C], BF16)
        sb("brow_bf", [DS, C], BF16)
        sb("crow_bf", [DS, C], BF16)
        for par in range(2):
            sb(f"b_bc{par}", [128, C], BF16)
            sb(f"c_bc{par}", [128, C], BF16)
        for par in range(2):
            sb(f"dt_f{par}", [128, C], F32)
            sb(f"dtx{par}", [128, C], BF16)
            sb(f"y_sb{par}", [128, C], F32)
        sb("u_sb", [128, C], F32)
        sb("hs_sb", [128, C], F32)
        sb("ys", [128, KI * C], BF16)
        sb("h_all", [128, KI * DS], F32)
        for par in range(2):
            sb(f"wout_slot{par}", [128, KI * 128], BF16)
        sb("out_acc", [128, KD * C], F32)
        for par in range(2):
            sb(f"wg_slot{par}", [128, KD * 256], BF16)
            sb(f"wu_slot{par}", [128, KD * 256], BF16)
            sb(f"gs_sb{par}", [128, C], BF16)
            sb(f"gs_c{par}", [128, C], BF16)
            sb(f"abar{par}", [128, C], F32)
        for par in range(4):
            sb(f"wd_slot{par}", [128, D], BF16)
        sb("h_half", [128, 14 * C], BF16)
        for j in range(8):
            self.P.append(nc.alloc_psum_tensor(f"ps{j}", [128, C], F32))

    def mm_group(self, mms, reads, writes):
        def fn(e):
            last = None
            for out, lhsT, rhs, st, sp in mms:
                last = e.matmul(out, lhsT, rhs, start=st, stop=sp)
            return last
        self.p.op("tensor", fn, reads=reads, writes=writes)

    def xcol(self, name, i, w=C):
        return self.S[name][:, i * w:(i + 1) * w]

    def prologue(self):
        p, S = self.p, self.S

        wi4_dst = self.wi_bf.rearrange("(g p) (k m) -> g p k m", p=128, m=128)
        wi4_src = self.w_inT.rearrange("(k p) (g m) -> g p k m", p=128, m=128)
        for g in range(32):
            p.op("qgpsimd", lambda e, g=g: [e.dma_start(out=wi4_dst[g], in_=wi4_src[g])],
                 writes=("hbm_wi",), chan="wi")
        wo4_dst = self.wo_bf.rearrange("(m p) (k c) -> m p k c", p=128, c=128)
        wo4_src = self.w_outT.rearrange("(k p) (m c) -> m p k c", p=128, c=128)
        for mm_ in range(KD):
            p.op("qgpsimd", lambda e, mm_=mm_: [e.dma_start(out=wo4_dst[mm_], in_=wo4_src[mm_])],
                 writes=("hbm_wo",), chan="wo")
        for e_ in range(E):
            g_dst = self.wg_bf[e_].rearrange("(s p) (k m) -> s p k m", p=128, m=256)
            g_src = self.wg_in[e_].rearrange("(k p) (s m) -> s p k m", p=128, m=256)
            u_dst = self.wu_bf[e_].rearrange("(s p) (k m) -> s p k m", p=128, m=256)
            u_src = self.wu_in[e_].rearrange("(k p) (s m) -> s p k m", p=128, m=256)
            for s in range(14):
                p.op("qgpsimd", lambda e, s=s, g_dst=g_dst, g_src=g_src: [
                    e.dma_start(out=g_dst[s], in_=g_src[s])],
                    writes=(f"hbm_cg{e_}",), chan=f"cg{e_}")
                p.op("qgpsimd", lambda e, s=s, u_dst=u_dst, u_src=u_src: [
                    e.dma_start(out=u_dst[s], in_=u_src[s])],
                    writes=(f"hbm_cu{e_}",), chan=f"cu{e_}")
            p.op("qgpsimd", lambda e, e_=e_: [e.dma_start(out=self.wd_bf[e_][:], in_=self.wd_in[e_][:])],
                 writes=(f"hbm_cd{e_}",), chan=f"cd{e_}")

        loads = [("ident", self.ident_in), ("sel", self.sel_in), ("conv_w", self.conv_w_l),
                 ("conv_b", self.conv_b_l), ("Dvec", self.Dvec_l), ("bdt", self.bdt_l),
                 ("alog_sb", self.alog)]
        for nm, src in loads:
            p.op("qsync", lambda e, nm=nm, src=src: [e.dma_start(out=S[nm][:], in_=src[:])],
                 writes=(nm,), chan="cst_" + nm)
        p.op("qsync", lambda e: [e.dma_start(out=S["out_acc"][:, 0:KI * 96], in_=self.w_xT_l[:])],
             writes=(("oa", 0), ("oa", 1), ("oa", 2)), chan="cst_wx")
        p.op("qsync", lambda e: [e.dma_start(out=S["out_acc"][0:DTR, KI * 96:KI * 96 + DI], in_=self.w_dtT[:])],
             writes=(("oa", 3), ("oa", 4), ("oa", 5), ("oa", 6)), chan="cst_wdt")
        p.op("qsync", lambda e: [e.dma_start(out=S["out_acc"][:, 3648:3648 + KD * E], in_=self.rw_l[:])],
             writes=(("oa", 7),), chan="cst_rw")
        p.op("vector", lambda e: e.tensor_copy(S["w_x_sb"][:], S["out_acc"][:, 0:KI * 96]),
             reads=(("oa", 0), ("oa", 1), ("oa", 2)), writes=("w_x_sb",))
        p.op("vector", lambda e: e.tensor_copy(S["w_dt_sb"][:], S["out_acc"][0:DTR, KI * 96:KI * 96 + DI]),
             reads=(("oa", 3), ("oa", 4), ("oa", 5), ("oa", 6)), writes=("w_dt_sb",))
        p.op("vector", lambda e: e.tensor_copy(S["rw_sb"][:], S["out_acc"][:, 3648:3648 + KD * E]),
             reads=(("oa", 7),), writes=("rw_sb",))

        p.op("vector", lambda e: e.memset(S["ones_row_f"][:], 1.0), writes=("ones_row_f",))
        p.op("vector", lambda e: e.memset(S["ones_col_bf"][:], 1.0), writes=("ones_col_bf",))
        p.op("vector", lambda e: e.memset(S["cconst"][:, 0:1], EPS), writes=("cconst",))
        p.op("vector", lambda e: e.memset(S["cconst"][:, 1:2], 1.0), writes=("cconst",))
        p.op("vector", lambda e: e.memset(S["cconst"][:, 2:3], -0.5), writes=("cconst",))
        p.op("vector", lambda e: e.memset(S["mpad1"][:], 1e30), writes=("mpad1",))
        p.op("vector", lambda e: e.memset(S["mpad2"][:], 1e30), writes=("mpad2",))
        p.op("gpsimd", lambda e: e.memset(S["h_all"][:], 0.0),
             writes=tuple(("h", i, s) for i in range(KI) for s in range(DS)))
        p.op("gpsimd", lambda e: e.memset(S["xr"][:], 0.0),
             writes=tuple(("xr", i) for i in range(KI)))

        p.op("scalar", lambda e: e.activation(S["alog_sb"][:], S["alog_sb"][:], AF.Exp),
             reads=("alog_sb",), writes=("alog_sb",))
        p.op("tensor", lambda e: e.matmul(self.P[0][:, 0:DS], S["ones_row_f"][:], S["alog_sb"][:],
                                          start=True, stop=True),
             reads=("ones_row_f", "alog_sb"), writes=("ps0",))
        p.op("vector", lambda e: e.tensor_scalar(S["a_bc"][:], self.P[0][:, 0:DS], -1.0, None, ALU.mult),
             reads=("ps0",), writes=("a_bc",))

    def ln_phase(self, n):
        p, S, P = self.p, self.S, self.P
        t0 = n * C
        xnb = f"xnb{n % 2}"

        p.op("qsync", lambda e: [e.dma_start(
            out=S["out_acc"][:],
            in_=self.x_til[n * 128:(n + 1) * 128, :])],
            writes=tuple(("oa", k) for k in range(KD)), chan="x")

        for k in range(KD):
            xbp = "xb0"
            p.op("vector", lambda e, k=k, xbp=xbp: e.tensor_copy(S[xbp][:], self.xcol("out_acc", k)),
                 reads=(("oa", k),), writes=(xbp,))
            self.mm_group([(P[0][0:1, :], S["ones_col_bf"][:], S[xbp][:], k == 0, k == KD - 1)],
                          reads=(xbp, "ones_col_bf"), writes=("ps0",))
            p.op("scalar", lambda e, xbp=xbp: e.activation(S[xbp][:], S[xbp][:], AF.Square),
                 reads=(xbp,), writes=(xbp,))
            self.mm_group([(P[1][0:1, :], S["ones_col_bf"][:], S[xbp][:], k == 0, k == KD - 1)],
                          reads=(xbp, "ones_col_bf"), writes=("ps1",))
        p.op("vector", lambda e: e.tensor_scalar(S["m_row"][:], P[0][0:1, :], 1.0 / D, None, ALU.mult),
             reads=("ps0",), writes=("m_row",))
        p.op("vector", lambda e: e.tensor_mul(S["var_row"][:], S["m_row"][:], S["m_row"][:]),
             reads=("m_row",), writes=("var_row",))
        p.op("vector", lambda e: e.scalar_tensor_tensor(S["var_row"][:], P[1][0:1, :], 1.0 / D,
                                                        S["var_row"][:], ALU.mult, ALU.subtract),
             reads=("ps1", "var_row"), writes=("var_row",))
        p.op("scalar", lambda e: e.activation(S["rs_row"][:], S["var_row"][:], AF.Ln,
                                              bias=S["cconst"][0:1, 0:1]),
             reads=("var_row", "cconst"), writes=("rs_row",))
        p.op("scalar", lambda e: e.activation(S["rs_row"][:], S["rs_row"][:], AF.Exp,
                                              scale=S["cconst"][0:1, 2:3]),
             reads=("rs_row", "cconst"), writes=("rs_row",))
        p.op("vector", lambda e: e.scalar_tensor_tensor(S["mrs_row"][:], S["m_row"][:], -1.0,
                                                        S["rs_row"][:], ALU.mult, ALU.mult),
             reads=("m_row", "rs_row"), writes=("mrs_row",))
        p.op("tensor", lambda e: e.matmul(P[2][:], S["ones_row_f"][:], S["rs_row"][:], start=True, stop=True),
             reads=("ones_row_f", "rs_row"), writes=("ps2",))
        p.op("tensor", lambda e: e.matmul(P[3][:], S["ones_row_f"][:], S["mrs_row"][:], start=True, stop=True),
             reads=("ones_row_f", "mrs_row"), writes=("ps3",))
        for k in range(KD):
            p.op("vector", lambda e, k=k: e.tensor_tensor(S["lntmp"][:], self.xcol("out_acc", k),
                                                          P[2][:], ALU.mult),
                 reads=(("oa", k), "ps2"), writes=("lntmp",))
            p.op("vector", lambda e, k=k: e.tensor_tensor(self.xcol(xnb, k), S["lntmp"][:],
                                                          P[3][:], ALU.add),
                 reads=("lntmp", "ps3"), writes=((xnb, k),))

    def router_phase(self, n):
        p, S, P = self.p, self.S, self.P
        xnb = f"xnb{n % 2}"
        self.mm_group([(P[4][0:E, :], S["rw_sb"][:, k * E:(k + 1) * E], self.xcol(xnb, k),
                        k == 0, k == KD - 1) for k in range(KD)],
                      reads=tuple((xnb, k) for k in range(KD)) + ("rw_sb",),
                      writes=("ps4",))
        p.op("scalar", lambda e: e.copy(S["lntmp"][0:E, :], P[4][0:E, :]), reads=("ps4",), writes=("lntmp",))
        for b in range(4):
            p.op("tensor", lambda e, b=b: e.transpose(P[5][:, 8 * b:8 * b + 8],
                                                      S["lntmp"][0:E, 128 * b:128 * (b + 1)],
                                                      S["ident"][0:E, 0:E]),
                 reads=("lntmp", "ident"), writes=("ps5",))
        p.op("vector", lambda e: e.tensor_copy(S["ltok"][:], P[5][:, 0:4 * E]),
             reads=("ps5",), writes=("ltok",))
        for b in range(4):
            sl = slice(8 * b, 8 * b + 8)
            cb = slice(b, b + 1)
            p.op("vector", lambda e, sl=sl, cb=cb: e.reduce_max(S["rm1"][:, cb], S["ltok"][:, sl], axis=AX.X),
                 reads=("ltok",), writes=("rm1",))
            p.op("vector", lambda e, cb=cb: e.tensor_copy(S["mpad1"][:, 0:1], S["rm1"][:, cb]),
                 reads=("rm1",), writes=("mpad1",))
            p.op("vector", lambda e, sl=sl: e.match_replace(S["l2t"][:, sl], S["mpad1"][:], S["ltok"][:, sl], -1e30),
                 reads=("mpad1", "ltok"), writes=("l2t",))
            p.op("vector", lambda e, sl=sl, cb=cb: e.reduce_max(S["rm2"][:, cb], S["l2t"][:, sl], axis=AX.X),
                 reads=("l2t",), writes=("rm2",))
            p.op("vector", lambda e, cb=cb: e.tensor_copy(S["mpad2"][:, 0:1], S["rm2"][:, cb]),
                 reads=("rm2",), writes=("mpad2",))
            p.op("vector", lambda e, sl=sl: e.match_replace(S["l3t"][:, sl], S["mpad2"][:], S["l2t"][:, sl], -1e30),
                 reads=("mpad2", "l2t"), writes=("l3t",))
        p.op("vector", lambda e: e.tensor_sub(S["rd_"][:], S["rm2"][:], S["rm1"][:]),
             reads=("rm2", "rm1"), writes=("rd_",))
        p.op("scalar", lambda e: e.activation(S["re_"][:], S["rd_"][:], AF.Exp),
             reads=("rd_",), writes=("re_",))
        p.op("vector", lambda e: e.tensor_scalar(S["rw1"][:], S["re_"][:], 1.0, None, ALU.add),
             reads=("re_",), writes=("rw1",))
        p.op("vector", lambda e: e.reciprocal(S["rw1"][:], S["rw1"][:]),
             reads=("rw1",), writes=("rw1",))
        p.op("vector", lambda e: e.tensor_scalar(S["rw2"][:], S["rw1"][:], -1.0, 1.0, ALU.mult, ALU.add),
             reads=("rw1",), writes=("rw2",))
        p.op("vector", lambda e: e.tensor_tensor(S["eq1"][:], S["ltok"][:], S["l2t"][:], ALU.not_equal),
             reads=("ltok", "l2t"), writes=("eq1",))
        p.op("vector", lambda e: e.tensor_tensor(S["eq2"][:], S["l2t"][:], S["l3t"][:], ALU.not_equal),
             reads=("l2t", "l3t"), writes=("eq2",))
        for b in range(4):
            sl = slice(8 * b, 8 * b + 8)
            cb = slice(b, b + 1)
            p.op("vector", lambda e, sl=sl, cb=cb: e.tensor_scalar(S["eq1"][:, sl], S["eq1"][:, sl],
                                                                   S["rw1"][:, cb], None, ALU.mult),
                 reads=("eq1", "rw1"), writes=("eq1",))
            p.op("vector", lambda e, sl=sl, cb=cb: e.tensor_scalar(S["eq2"][:, sl], S["eq2"][:, sl],
                                                                   S["rw2"][:, cb], None, ALU.mult),
                 reads=("eq2", "rw2"), writes=("eq2",))
        p.op("vector", lambda e: e.tensor_add(S["cmb_tok"][:], S["eq1"][:], S["eq2"][:]),
             reads=("eq1", "eq2"), writes=("cmb_tok",))
        for b in range(4):
            p.op("tensor", lambda e, b=b: e.transpose(P[6][0:E, 128 * b:128 * (b + 1)],
                                                      S["cmb_tok"][:, 8 * b:8 * b + 8],
                                                      S["ident"][:]),
                 reads=("cmb_tok", "ident"), writes=("ps6",))
        p.op("scalar", lambda e: e.copy(S["cmb_rows_bf"][:], P[6][0:E, :]),
             reads=("ps6",), writes=("cmb_rows_bf",))

    def front_phase(self, n):
        p, S, P = self.p, self.S, self.P
        xnb = f"xnb{n % 2}"
        if n > 0:
            for i in range(KI):
                o = i * (C + 3)
                p.op("gpsimd", lambda e, o=o: e.tensor_copy(S["xr"][:, o:o + 3], S["xr"][:, o + C:o + C + 3]),
                     reads=(("xr", i),), writes=(("xr", i),))
        for mt in range(32):
            par = mt % 2
            bank = mt % 4
            p.op("qsync", lambda e, mt=mt, par=par: [e.dma_start(
                out=S[f"win_slot{par}"][:],
                in_=self.wi_bf[mt * 128:(mt + 1) * 128, :])],
                reads=("hbm_wi",), writes=(("win", par),), chan=f"win{par}")
            self.mm_group([(P[bank][:], S[f"win_slot{par}"][:, k * 128:(k + 1) * 128],
                            self.xcol(xnb, k), k == 0, k == KD - 1) for k in range(KD)],
                          reads=tuple((xnb, k) for k in range(KD)) + (("win", par),),
                          writes=(f"ps{bank}",))
            if mt < KI:
                o = mt * (C + 3)
                p.op("scalar", lambda e, mt=mt, bank=bank, o=o: e.copy(S["xr"][:, o + 3:o + 3 + C], P[bank][:]),
                     reads=(f"ps{bank}",), writes=(("xr", mt),))
            else:
                zt = mt - KI
                p.op("scalar", lambda e, zt=zt, bank=bank: e.activation(self.xcol("zs", zt), P[bank][:], AF.Silu),
                     reads=(f"ps{bank}",), writes=(("zs", zt),))
    def conv_unit(self, i):
        p, S = self.p, self.S
        o = i * (C + 3)
        ca = "convacc0"
        p.op("vector", lambda e, o=o, i=i, ca=ca: e.tensor_scalar(
            S[ca][:], S["xr"][:, o:o + C], S["conv_w"][:, i * DC:i * DC + 1], None, ALU.mult),
            reads=(("xr", i), "conv_w"), writes=(ca,))
        for kk in range(1, DC):
            p.op("vector", lambda e, o=o, i=i, kk=kk, ca=ca: e.scalar_tensor_tensor(
                S[ca][:], S["xr"][:, o + kk:o + kk + C], S["conv_w"][:, i * DC + kk:i * DC + kk + 1],
                S[ca][:], ALU.mult, ALU.add),
                reads=(("xr", i), "conv_w", ca), writes=(ca,))
        p.op("scalar", lambda e, i=i, ca=ca: e.activation(self.xcol("xs", i), S[ca][:], AF.Silu,
                                                          bias=S["conv_b"][:, i:i + 1]),
             reads=(ca, "conv_b"), writes=(("xs", i),))

    def xdbl_unit(self):
        p, S, P = self.p, self.S, self.P
        xs_reads = tuple(("xs", i) for i in range(KI))
        self.mm_group([(P[4][0:DTR, :], S["w_x_sb"][:, k * 96:k * 96 + DTR], self.xcol("xs", k),
                        k == 0, k == KI - 1) for k in range(KI)],
                      reads=xs_reads + ("w_x_sb",), writes=("ps4",))
        self.mm_group([(P[5][0:DS, :], S["w_x_sb"][:, k * 96 + DTR:k * 96 + DTR + DS], self.xcol("xs", k),
                        k == 0, k == KI - 1) for k in range(KI)],
                      reads=xs_reads + ("w_x_sb",), writes=("ps5",))
        self.mm_group([(P[5][32:32 + DS, :], S["w_x_sb"][:, k * 96 + 80:k * 96 + 96], self.xcol("xs", k),
                        k == 0, k == KI - 1) for k in range(KI)],
                      reads=xs_reads + ("w_x_sb",), writes=("ps5",))
        p.op("scalar", lambda e: e.copy(S["dtr_b"][:], P[4][0:DTR, :]), reads=("ps4",), writes=("dtr_b",))
        p.op("scalar", lambda e: e.copy(S["brow_bf"][:], P[5][0:DS, :]), reads=("ps5",), writes=("brow_bf",))
        p.op("scalar", lambda e: e.copy(S["crow_bf"][:], P[5][32:32 + DS, :]), reads=("ps5",), writes=("crow_bf",))

    def scan_dt_unit(self, pr):
        p, S, P = self.p, self.S, self.P
        for i in (2 * pr, 2 * pr + 1):
            par = i % 2
            p.op("tensor", lambda e, i=i: e.matmul(P[7][:], S["w_dt_sb"][:, 128 * i:128 * (i + 1)],
                                                   S["dtr_b"][:], start=True, stop=True),
                 reads=("w_dt_sb", "dtr_b"), writes=("ps7",))
            p.op("scalar", lambda e, i=i, par=par: e.activation(S[f"dt_f{par}"][:], P[7][:], AF.Exp,
                                                                bias=S["bdt"][:, i:i + 1]),
                 reads=("ps7", "bdt"), writes=(f"dt_f{par}",))
            p.op("scalar", lambda e, par=par: e.activation(S[f"dt_f{par}"][:], S[f"dt_f{par}"][:], AF.Ln,
                                                           bias=S["cconst"][:, 1:2]),
                 reads=(f"dt_f{par}", "cconst"), writes=(f"dt_f{par}",))
            p.op("vector", lambda e, i=i, par=par: e.tensor_tensor(S[f"dtx{par}"][:], S[f"dt_f{par}"][:],
                                                                   self.xcol("xs", i), ALU.mult),
                 reads=(f"dt_f{par}", ("xs", i)), writes=(f"dtx{par}",))
            p.op("vector", lambda e, i=i, par=par: e.tensor_scalar(S[f"y_sb{par}"][:], self.xcol("xs", i),
                                                                   S["Dvec"][:, i:i + 1], None, ALU.mult),
                 reads=(("xs", i), "Dvec"), writes=(f"y_sb{par}",))

    def scan_s_unit(self, pr, s):
        p, S, P = self.p, self.S, self.P
        sp = s % 2
        p.op("tensor", lambda e, s=s: e.matmul(P[5][:], S["sel"][:, 128 * s:128 * (s + 1)],
                                               S["brow_bf"][:], start=True, stop=True),
             reads=("sel", "brow_bf"), writes=("ps5",))
        p.op("scalar", lambda e, sp=sp: e.copy(S[f"b_bc{sp}"][:], P[5][:]),
             reads=("ps5",), writes=(f"b_bc{sp}",))
        p.op("tensor", lambda e, s=s: e.matmul(P[6][:], S["sel"][:, 128 * s:128 * (s + 1)],
                                               S["crow_bf"][:], start=True, stop=True),
             reads=("sel", "crow_bf"), writes=("ps6",))
        p.op("scalar", lambda e, sp=sp: e.copy(S[f"c_bc{sp}"][:], P[6][:]),
             reads=("ps6",), writes=(f"c_bc{sp}",))
        for i in (2 * pr, 2 * pr + 1):
            par = i % 2
            hkey = ("h", i, s)
            hcol = i * DS + s
            ab = S[f"abar{par}"]
            abk = f"abar{par}"
            p.op("scalar", lambda e, par=par, s=s, ab=ab: e.activation(ab[:], S[f"dt_f{par}"][:], AF.Exp,
                                                                       scale=S["a_bc"][:, s:s + 1]),
                 reads=(f"dt_f{par}", "a_bc"), writes=(abk,))
            p.op("vector", lambda e, par=par, sp=sp: e.tensor_tensor(S["u_sb"][:], S[f"dtx{par}"][:],
                                                                     S[f"b_bc{sp}"][:], ALU.mult),
                 reads=(f"dtx{par}", f"b_bc{sp}"), writes=("u_sb",))
            p.op("vector", lambda e, par=par, hcol=hcol, ab=ab: e.tensor_tensor_scan(
                S["hs_sb"][:], ab[:], S["u_sb"][:], S["h_all"][:, hcol:hcol + 1],
                ALU.mult, ALU.add),
                reads=(abk, "u_sb", hkey), writes=("hs_sb",))
            p.op("gpsimd", lambda e, hcol=hcol: e.tensor_copy(S["h_all"][:, hcol:hcol + 1], S["hs_sb"][:, C - 1:C]),
                 reads=("hs_sb",), writes=(hkey,))
            p.op("vector", lambda e, sp=sp: e.tensor_tensor(S["u_sb"][:], S["hs_sb"][:],
                                                            S[f"c_bc{sp}"][:], ALU.mult),
                 reads=("hs_sb", f"c_bc{sp}"), writes=("u_sb",))
            p.op("vector", lambda e, par=par: e.tensor_tensor(S[f"y_sb{par}"][:], S[f"y_sb{par}"][:],
                                                              S["u_sb"][:], ALU.add),
                 reads=(f"y_sb{par}", "u_sb"), writes=(f"y_sb{par}",))

    def scan_ys_unit(self, pr):
        p, S = self.p, self.S
        for i in (2 * pr, 2 * pr + 1):
            par = i % 2
            p.op("vector", lambda e, i=i, par=par: e.tensor_tensor(self.xcol("ys", i), S[f"y_sb{par}"][:],
                                                                   self.xcol("zs", i), ALU.mult),
                 reads=(f"y_sb{par}", ("zs", i)), writes=(("ys", i),))

    def moe_pre_unit(self, n, e_):
        p, S, P = self.p, self.S, self.P
        par = e_ % 2
        p.op("tensor", lambda e, e_=e_: e.matmul(P[4][:], S["sel"][0:E, 128 * e_:128 * (e_ + 1)],
                                                 S["cmb_rows_bf"][:], start=True, stop=True),
             reads=("sel", "cmb_rows_bf"), writes=("ps4",))
        p.op("scalar", lambda e, par=par: e.copy(S[f"cmb_sb{par}"][:], P[4][:]),
             reads=("ps4",), writes=(f"cmb_sb{par}",))

    def moe_gu_unit(self, n, e_, h, st):
        p, S, P = self.p, self.S, self.P
        xnb = f"xnb{n % 2}"
        par = st % 2
        sidx = h * 7 + st
        p.op("qsync", lambda e, sidx=sidx, par=par, e_=e_: [e.dma_start(
            out=S[f"wg_slot{par}"][:],
            in_=self.wg_bf[e_][sidx * 128:(sidx + 1) * 128, :])],
            reads=("hbm_cg%d" % e_,), writes=(("wgs", par),), chan=f"wg{par}")
        p.op("qsync", lambda e, sidx=sidx, par=par, e_=e_: [e.dma_start(
            out=S[f"wu_slot{par}"][:],
            in_=self.wu_bf[e_][sidx * 128:(sidx + 1) * 128, :])],
            reads=("hbm_cu%d" % e_,), writes=(("wus", par),), chan=f"wu{par}")
        for j in range(2):
            ftl = st * 2 + j
            self.mm_group([(P[j][:], S[f"wg_slot{par}"][:, k * 256 + 128 * j:k * 256 + 128 * (j + 1)],
                            self.xcol(xnb, k), k == 0, k == KD - 1) for k in range(KD)],
                          reads=tuple((xnb, k) for k in range(KD)) + (("wgs", par),),
                          writes=(f"ps{j}",))
            self.mm_group([(P[2 + j][:], S[f"wu_slot{par}"][:, k * 256 + 128 * j:k * 256 + 128 * (j + 1)],
                            self.xcol(xnb, k), k == 0, k == KD - 1) for k in range(KD)],
                          reads=tuple((xnb, k) for k in range(KD)) + (("wus", par),),
                          writes=(f"ps{2 + j}",))
            p.op("scalar", lambda e, j=j: e.activation(S[f"gs_sb{j}"][:], P[j][:], AF.Silu),
                 reads=(f"ps{j}",), writes=(f"gs_sb{j}",))
            cpar = e_ % 2
            p.op("vector", lambda e, j=j, cpar=cpar: e.tensor_tensor(S[f"gs_c{j}"][:], S[f"gs_sb{j}"][:],
                                                                     S[f"cmb_sb{cpar}"][:], ALU.mult),
                 reads=(f"gs_sb{j}", f"cmb_sb{cpar}"), writes=(f"gs_c{j}",))
            p.op("vector", lambda e, j=j, ftl=ftl: e.tensor_tensor(self.xcol("h_half", ftl), S[f"gs_c{j}"][:],
                                                                   P[2 + j][:], ALU.mult),
                 reads=(f"gs_c{j}", f"ps{2 + j}"), writes=(("hh", ftl),))

    def moe_down_units(self, e_, h):
        p, S, P = self.p, self.S, self.P
        def wd_dma(ks, e_=e_, h=h):
            par = ks % 4
            s0 = h * 14 + ks
            p.op("qsync", lambda e, s0=s0, par=par, e_=e_: [e.dma_start(
                out=S[f"wd_slot{par}"][:],
                in_=self.wd_bf[e_][128 * s0:128 * (s0 + 1), :])],
                reads=("hbm_cd%d" % e_,), writes=(("wds", par),), chan=f"wd{par}")

        units = []
        for ks in range(14):
            def unit(ks=ks, e_=e_, h=h):
                if ks == 0:
                    for kpre in range(4):
                        wd_dma(kpre)
                par = ks % 4
                for grp in range(2):
                    mlist = range(4 * grp, 4 * grp + 4)
                    self.mm_group(
                        [(P[m][:], S[f"wd_slot{par}"][:, 128 * m:128 * (m + 1)],
                          self.xcol("h_half", ks), ks == 0, ks == 13)
                         for m in mlist],
                        reads=(("wds", par), ("hh", ks)),
                        writes=tuple(f"ps{m}" for m in mlist))
                if ks + 4 < 14:
                    wd_dma(ks + 4)
            units.append(unit)

        def evac(e_=e_, h=h):
            for m in range(8):
                p.op("vector", lambda e, m=m: e.tensor_tensor(self.xcol("out_acc", m), self.xcol("out_acc", m),
                                                              P[m][:], ALU.add),
                     reads=(("oa", m), f"ps{m}"), writes=(("oa", m),))
        units.append(evac)
        return units

    def wout_phase(self, n):
        p, S, P = self.p, self.S, self.P
        for m in range(KD):
            par = m % 2
            p.op("qsync", lambda e, m=m, par=par: [e.dma_start(
                out=S[f"wout_slot{par}"][:],
                in_=self.wo_bf[m * 128:(m + 1) * 128, :])],
                reads=("hbm_wo",), writes=(("wos", par),), chan=f"wout{par}")
            self.mm_group([(P[par][:], S[f"wout_slot{par}"][:, k * 128:(k + 1) * 128],
                            self.xcol("ys", k), k == 0, k == KI - 1) for k in range(KI)],
                          reads=tuple(("ys", k) for k in range(KI)) + (("wos", par),),
                          writes=(f"ps{par}",))
            p.op("vector", lambda e, m=m, par=par: e.tensor_tensor(self.xcol("out_acc", m),
                                                                   self.xcol("out_acc", m), P[par][:], ALU.add),
                 reads=(("oa", m), f"ps{par}"), writes=(("oa", m),))

    def store_phase(self, n):
        p, S = self.p, self.S
        p.op("qsync", lambda e: [e.dma_start(
            out=self.out_til[n * 128:(n + 1) * 128, :],
            in_=S["out_acc"][:])],
            reads=tuple(("oa", k) for k in range(KD)), writes=("stored",), chan="st")

    def chunk(self, n):
        self.ln_phase(n)
        self.router_phase(n)
        self.front_phase(n)
        mamba_units = [lambda i=i: self.conv_unit(i) for i in range(KI)]
        mamba_units.append(lambda: self.xdbl_unit())
        for pr in range(E):
            mamba_units.append(lambda pr=pr: self.scan_dt_unit(pr))
            for s in range(DS):
                mamba_units.append(lambda pr=pr, s=s: self.scan_s_unit(pr, s))
            mamba_units.append(lambda pr=pr: self.scan_ys_unit(pr))
        moe_units = []
        for e_ in range(E):
            moe_units.append(lambda e_=e_: self.moe_pre_unit(n, e_))
            for h in range(2):
                for st in range(7):
                    moe_units.append(lambda e_=e_, h=h, st=st: self.moe_gu_unit(n, e_, h, st))

                def down_block(e_=e_, h=h):
                    for u in self.moe_down_units(e_, h):
                        u()
                moe_units.append(down_block)
        na, nb = len(mamba_units), len(moe_units)
        ia = ib = 0
        while ia < na or ib < nb:
            if ib * na <= ia * nb and ib < nb:
                moe_units[ib]()
                ib += 1
            elif ia < na:
                mamba_units[ia]()
                ia += 1
            else:
                moe_units[ib]()
                ib += 1
        self.wout_phase(n)
        self.store_phase(n)

    def build(self):
        self.prologue()
        for n in range(self.NCH):
            self.chunk(n)
        self.p.final_wait("sync", "st", self.p.barrier_val("st"))
        self.p.render()



def _host_consts():
    ident = np.eye(128, dtype=np.float32)
    sel = np.zeros((DS, DS * 128), dtype=np.float32)
    for s in range(DS):
        sel[s, 128 * s:128 * (s + 1)] = 1.0
    sel = sel.astype(ml_dtypes.bfloat16)
    return ident, sel


def _host_prep(inputs, T):
    W_in = inputs["L0_attn_W_in"]
    conv_w = inputs["L0_attn_conv_w"]
    conv_b = inputs["L0_attn_conv_b"]
    A_log = inputs["L0_attn_A_log"]
    Dv = inputs["L0_attn_D"]
    W_x = inputs["L0_attn_W_x"]
    W_dt = inputs["L0_attn_W_dt"]
    b_dt = inputs["L0_attn_b_dt"]
    W_out = inputs["L0_attn_W_out"]
    router_w = inputs["L0_ffn_router_w"]
    wg = inputs["L0_ffn_w_gate"]
    wu = inputs["L0_ffn_w_up"]
    wd = inputs["L0_ffn_w_down"]
    ident, sel = _host_consts()

    f32 = np.float32
    shared = {
        "w_inT": np.ascontiguousarray(W_in.T, dtype=f32),
        "w_outT": np.ascontiguousarray(W_out.T, dtype=f32),
        "w_xT_l": np.ascontiguousarray(
            W_x.T.reshape(KI, 128, 96).transpose(1, 0, 2).reshape(128, KI * 96), dtype=f32),
        "w_dtT": np.ascontiguousarray(W_dt.T, dtype=f32),
        "rw_l": np.ascontiguousarray(
            router_w.T.reshape(KD, 128, E).transpose(1, 0, 2).reshape(128, KD * E), dtype=f32),
        "conv_w_l": np.ascontiguousarray(
            conv_w.reshape(KI, 128, DC).transpose(1, 0, 2).reshape(128, KI * DC), dtype=f32),
        "conv_b_l": np.ascontiguousarray(conv_b.reshape(KI, 128).T, dtype=f32),
        "Dvec_l": np.ascontiguousarray(Dv.reshape(KI, 128).T, dtype=f32),
        "bdt_l": np.ascontiguousarray(b_dt.reshape(KI, 128).T, dtype=f32),
        "alog": np.ascontiguousarray(A_log.reshape(1, DS), dtype=f32),
        "ident_in": ident,
        "sel_in": sel,
    }
    for e in range(E):
        shared[f"wg{e}"] = np.ascontiguousarray(wg[e], dtype=f32)
        shared[f"wu{e}"] = np.ascontiguousarray(wu[e], dtype=f32)
        shared[f"wd{e}"] = np.ascontiguousarray(wd[e], dtype=f32)
    return shared


def kernel(**inputs):
    x = np.asarray(inputs["L0_attn_x"])
    B, T, _ = x.shape
    assert B == 8 and T % C == 0

    nc = bass.Bass("TRN2", target_bir_lowering=False)
    mk = MK(nc, T)
    mk.build()

    shared = _host_prep(inputs, T)
    nch = T // C
    in_maps = []
    for b in range(B):
        m = dict(shared)
        m["x_til"] = _tile_x(x[b], nch)
        in_maps.append(m)

    res = run_bass_kernel_spmd(nc, in_maps, core_ids=list(range(B)))
    out = np.stack([_untile_out(np.asarray(res.results[b]["out_til"]), nch, T)
                    for b in range(B)], axis=0)
    return np.ascontiguousarray(out, dtype=np.float32)


def _tile_x(xb, nch):
    xT = xb.T
    return np.ascontiguousarray(
        xT.reshape(KD, 128, nch, C).transpose(2, 1, 0, 3).reshape(nch * 128, KD * C))


def _untile_out(ot, nch, T):
    outT = ot.reshape(nch, 128, KD, C).transpose(2, 1, 0, 3).reshape(D, T)
    return outT.T
